# version 14
# baseline (speedup 1.0000x reference)
import numpy as np
import ml_dtypes

import concourse.bass as bass
import concourse.mybir as mybir
from concourse.bass_utils import run_bass_kernel_spmd

F32 = mybir.dt.float32
BF16 = mybir.dt.bfloat16
AF = mybir.ActivationFunctionType
ALU = mybir.AluOpType
AX = mybir.AxisListType

B, T, D = 2, 4096, 2048
KVD = 512
FH = 5632
NBLK = 8
BLK = 512
NTOK = 1024
EPS_LN = 1e-5
EPS_Q = 1e-8


def perm_blocks(j):
    return [j, 7 - j] + [b for b in range(NBLK) if b not in (j, 7 - j)]


def qrowperm():
    idx = []
    for t in range(16):
        a, g = t // 4, t % 4
        for hh in (2 * a, 2 * a + 1):
            base = hh * 256 + g * 64
            idx.extend(range(base, base + 64))
    return np.array(idx, dtype=np.int64)


class Sem:
    def __init__(self, h):
        self.h = h
        self.n = 0

    def inc(self, instr, k=1):
        if instr is not None:
            instr.then_inc(self.h, k)
        self.n += k
        return self.n


class KB:
    NELS = 101376
    R = dict(
        kt=(0, 16384),
        vx=(16384, 8448),
        vy=(24832, 16384),
        qt=(41216, 16384),
        xT=(57600, 16384),
        yt=(57600, 16384),
        x1T=(57600, 16384),
        wkv=(73984, 8192),
        wvh=(82176, 4096),
        expx=(73984, 4096),
        woT=(73984, 8192),
        lnd=(73984, 4096),
        w1r=(73984, 8192),
        w12f=(73984, 8192),
        wqs=(82176, 4096),
        w12q=(82176, 4096),
        gma=(82176, 4096),
        consts=(86272, 5248),
        zacc=(0, 32768),
        x1tok=(32768, 16384),
        hidden=(49152, 8192),
        scr=(91520, 9856),
        outs=(49152, 4096),
    )
    SCR = dict(
        w12qB=(0, 4096),
        kpre=(0, 1024),
        cs=(1024, 4096),
        ln=(5120, 64),
        m12=(5184, 2048),
        sig=(7232, 1024),
        gbuf=(8256, 640),
        tv=(8896, 64),
        stt=(8960, 640),
        w3f=(9600, 256),
    )
    CO = dict(ident=0, P=128, masks=256, ones=2304, bo=2432, bv=4480)

    def __init__(self, dbg_phase=0):
        self.dbg = dbg_phase
        nc = bass.Bass("TRN2", target_bir_lowering=False)
        self.nc = nc
        self.d_xt = nc.dram_tensor("xt", [D, T], F32, kind="ExternalInput").ap()
        self.d_xo = nc.dram_tensor("xo", [NTOK, D], F32, kind="ExternalInput").ap()
        self.d_cos = nc.dram_tensor("cos_t", [128, T], F32, kind="ExternalInput").ap()
        self.d_sin = nc.dram_tensor("sin_t", [128, T], F32, kind="ExternalInput").ap()
        self.d_wqt = nc.dram_tensor("wqt", [D, D], F32, kind="ExternalInput").ap()
        self.d_wkt = nc.dram_tensor("wkt", [D, KVD], F32, kind="ExternalInput").ap()
        self.d_wvt = nc.dram_tensor("wvt", [D, KVD], F32, kind="ExternalInput").ap()
        self.d_wot = nc.dram_tensor("wot", [D, D], F32, kind="ExternalInput").ap()
        self.d_w1t = nc.dram_tensor("w1t", [D, FH], F32, kind="ExternalInput").ap()
        self.d_w2t = nc.dram_tensor("w2t", [D, FH], F32, kind="ExternalInput").ap()
        self.d_w3t = nc.dram_tensor("w3t", [FH, D], F32, kind="ExternalInput").ap()
        self.d_bqt = nc.dram_tensor("bq_t", [128, 16], F32, kind="ExternalInput").ap()
        self.d_bkt = nc.dram_tensor("bk_t", [128, 4], F32, kind="ExternalInput").ap()
        self.d_bo = nc.dram_tensor("bo_r", [1, D], F32, kind="ExternalInput").ap()
        self.d_bv = nc.dram_tensor("bv_r", [1, KVD], F32, kind="ExternalInput").ap()
        self.d_cbf = nc.dram_tensor("consts_bf", [128, 2432], F32, kind="ExternalInput").ap()
        self.d_cf = nc.dram_tensor("cf32", [128, 132], F32, kind="ExternalInput").ap()
        self.d_out = nc.dram_tensor("out", [NTOK, D], F32, kind="ExternalOutput").ap()
        if self.dbg:
            dt_ = nc.dram_tensor
            self.d_dbg_kt = dt_("dbg_kt", [128, 16384], BF16, kind="ExternalOutput").ap()
            self.d_dbg_vx = dt_("dbg_vx", [128, 8448], BF16, kind="ExternalOutput").ap()
            self.d_dbg_vy = dt_("dbg_vy", [128, 16384], BF16, kind="ExternalOutput").ap()
            self.d_dbg_qt = dt_("dbg_qt", [128, 16384], BF16, kind="ExternalOutput").ap()
            self.d_dbg_yt = dt_("dbg_yt", [128, 16384], BF16, kind="ExternalOutput").ap()
            self.d_dbg_z = dt_("dbg_z", [128, 16384], F32, kind="ExternalOutput").ap()
            self.d_dbg_x1 = dt_("dbg_x1", [128, 16384], BF16, kind="ExternalOutput").ap()
            self.d_dbg_x1t = dt_("dbg_x1t", [128, 16384], BF16, kind="ExternalOutput").ap()
            self.d_dbg_ln = dt_("dbg_ln", [128, 16], F32, kind="ExternalOutput").ap()
            self.d_dbg_hid = dt_("dbg_hid", [128, 8192], BF16, kind="ExternalOutput").ap()

    def A(self, region, off=0, n=None, p0=0, p1=128):
        base, sz = self.R[region]
        if n is None:
            n = sz - off
        assert off + n <= sz, (region, off, n, sz)
        return self.arn[p0:p1, base + off: base + off + n]

    def Af(self, region, off_f32=0, n_f32=None, p0=0, p1=128):
        base, sz = self.R[region]
        if n_f32 is None:
            n_f32 = sz // 2 - off_f32
        assert 2 * (off_f32 + n_f32) <= sz, (region, off_f32, n_f32)
        ap = self.arn[p0:p1, base + 2 * off_f32: base + 2 * (off_f32 + n_f32)]
        return ap.bitcast(F32)

    def S_(self, key, off=0, n=None, p0=0, p1=128):
        base, sz = self.SCR[key]
        if n is None:
            n = sz - off
        assert off + n <= sz, (key, off, n)
        o = self.R['scr'][0] + base
        return self.arn[p0:p1, o + off: o + off + n]

    def Sf(self, key, off_f32=0, n_f32=1, p0=0, p1=128):
        base, sz = self.SCR[key]
        assert 2 * (off_f32 + n_f32) <= sz, (key, off_f32, n_f32)
        o = self.R['scr'][0] + base
        ap = self.arn[p0:p1, o + 2 * off_f32: o + 2 * (off_f32 + n_f32)]
        return ap.bitcast(F32)

    def C_(self, key, n, off=0, p0=0, p1=128):
        return self.A('consts', self.CO[key] + off, n, p0, p1)

    def psum(self, bank, n_f32=512, off_f32=0, p0=0, p1=128):
        return self.psa[p0:p1, bank * 512 + off_f32: bank * 512 + off_f32 + n_f32]

    def psum_bf(self, bank, n_bf=1024, off_bf=0, p0=0, p1=128):
        assert off_bf % 2 == 0 and (off_bf + n_bf) <= 1024
        ap = self.psa[p0:p1, bank * 512 + off_bf // 2: bank * 512 + (off_bf + n_bf) // 2]
        return ap.bitcast(BF16)

    def build(self):
        nc = self.nc
        from contextlib import ExitStack
        with ExitStack() as st:
            self.arn_t = st.enter_context(nc.sbuf_tensor("ARN", [128, self.NELS], BF16))
            self.arn = self.arn_t[:]
            self.psa_t = st.enter_context(nc.psum_tensor("PSA", [128, 4096], F32))
            self.psa = self.psa_t[:]
            self.cf = st.enter_context(nc.sbuf_tensor("CF", [128, 132], F32))[:]
            self.bq = st.enter_context(nc.sbuf_tensor("BQ", [128, 16], F32))[:]
            self.bk = st.enter_context(nc.sbuf_tensor("BK", [128, 4], F32))[:]

            sems = {}
            for name in ["sXT", "sWKV", "sCS", "sWQ", "sCN", "sCNG",
                         "sPJ", "sKE", "sRP", "sRC", "sVE",
                         "aQKx", "aQKy", "aEXx", "aEXy", "aMK", "aAVx", "aAVy",
                         "aDN", "aBC", "aNM", "aDONE",
                         "sZI", "sWO", "sOP", "sZA", "sLN", "sX1E",
                         "sW12", "sQZ", "sHP", "sSG", "sHD",
                         "sW3", "sQ3", "sFP", "sZ2", "sL2", "sOUT", "sDBG"]:
                sems[name] = Sem(st.enter_context(nc.semaphore(name)))
            self.s = sems

            with nc.Block():
                self.emit_all()
        return nc

    def emit_all(self):
        self.emit_p0()
        self.emit_p1()
        if self.dbg == 1:
            self.emit_dbg1()
            return
        self.emit_p2()
        if self.dbg == 2:
            self.emit_dbg2()
            return
        self.emit_p3()
        if self.dbg == 3:
            self.emit_dbg3()
            return
        self.emit_p4()
        if self.dbg == 4:
            self.emit_dbg4()
            return
        self.emit_p5()

    def emit_p0(self):
        nc, s = self.nc, self.s
        SY, G = nc.sync, nc.gpsimd
        s["sCNG"].inc(G.dma_start(out=self.A('consts', 0, 2432), in_=self.d_cbf), 16)
        s["sCN"].inc(SY.dma_start(out=self.cf, in_=self.d_cf), 16)
        s["sCN"].inc(SY.dma_start(out=self.bq, in_=self.d_bqt), 16)
        s["sCN"].inc(SY.dma_start(out=self.bk, in_=self.d_bkt), 16)
        s["sCNG"].inc(G.dma_start(out=self.C_('bo', D, p0=0, p1=1), in_=self.d_bo), 16)
        s["sCNG"].inc(G.dma_start(out=self.C_('bv', KVD, p0=0, p1=1), in_=self.d_bv), 16)
        wk_src = self.d_wkt.rearrange("(f p) (u c) -> p u f c", p=128, c=128)
        wkdst = self.A('wkv', 0, 8192).rearrange("p (u f c) -> p u f c", u=4, f=16)
        s["sWKV"].inc(G.dma_start(out=wkdst, in_=wk_src), 16)

    def rope_evac(self, dst_ap, bias_col, pj_bank, cs_slot):
        nc, s = self.nc, self.s
        T, SC, V = nc.tensor, nc.scalar, nc.vector
        kslot = self.kslot_i % 2
        self.kslot_i += 1
        kpre = self.S_('kpre', kslot * 512, 512)
        if self.kpre_last[kslot]:
            SC.wait_ge(s["sRC"].h, self.kpre_last[kslot])
        SC.wait_ge(s["sPJ"].h, s["sPJ"].n)
        ke_n = s["sKE"].inc(SC.activation(kpre, self.psum(pj_bank), AF.Identity,
                                          bias=bias_col, scale=1.0))
        rbank = 6 + kslot
        T.wait_ge(s["sKE"].h, ke_n)
        if self.rp_last[kslot]:
            T.wait_ge(s["sRC"].h, self.rp_last[kslot])
        rp_n = s["sRP"].inc(T.matmul(self.psum(rbank), self.C_('P', 128),
                                     kpre, start=True, stop=True))
        V.wait_ge(s["sRP"].h, rp_n)
        V.wait_ge(s["sCS"].h, self.cs_need[cs_slot])
        m1 = self.Sf('m12', 0, 512)
        m2 = self.Sf('m12', 512, 512)
        cos = self.Sf('cs', cs_slot * 1024, 512)
        sin = self.Sf('cs', cs_slot * 1024 + 512, 512)
        V.tensor_tensor(m1, kpre, cos, op=ALU.mult)
        V.tensor_tensor(m2, self.psum(rbank), sin, op=ALU.mult)
        rc_n = s["sRC"].inc(V.tensor_tensor(dst_ap, m1, m2, op=ALU.add))
        self.rp_last[kslot] = rc_n
        self.kpre_last[kslot] = rc_n
        self.cs_last[cs_slot] = rc_n
        return rc_n

    def emit_p1(self):
        nc, s = self.nc, self.s
        T, SC, V, SY, G = nc.tensor, nc.scalar, nc.vector, nc.sync, nc.gpsimd
        self.rp_last = [0, 0]
        self.kpre_last = [0, 0]
        self.cs_last = [0, 0]
        self.cs_need = [0, 0]
        self.kslot_i = 0
        self.wq_used = [0, 0]
        self.wv_used = 0
        pj_evac = {}
        self.xt_free = [0, 0]
        xt_rr = self.A('xT').rearrange("p (f r) -> p f r", f=16)

        def xt_tile(f, slot, col0=0, ncol=512):
            base = self.R['xT'][0] + f * 1024 + slot * 512 + col0
            return self.arn[:, base: base + ncol]

        def proj_accum(bank, lhs, rhs, extra=None):
            if pj_evac.get(bank) is not None:
                sem, val = pj_evac[bank]
                T.wait_ge(sem.h, val)
            n = len(lhs)
            for i in range(n):
                last = (i == n - 1) and extra is None
                ins = T.matmul(self.psum(bank), lhs[i], rhs[i],
                               start=(i == 0), stop=last)
            if extra is not None:
                ins = T.matmul(self.psum(bank), extra[0], extra[1],
                               start=False, stop=True)
            return s["sPJ"].inc(ins)

        wkv_waited = False
        for c in range(8):
            slot = c % 2
            src = self.d_xt.rearrange("(f p) t -> p f t", p=128)[:, :, c * 512:(c + 1) * 512]
            dst = xt_rr[:, :, slot * 512:(slot + 1) * 512]
            G.wait_ge(s["sPJ"].h, self.xt_free[slot])
            xt_n = s["sXT"].inc(G.dma_start(out=dst, in_=src), 16)
            SY.wait_ge(s["sRC"].h, self.cs_last[slot])
            s["sCS"].inc(SY.dma_start(out=self.Sf('cs', slot * 1024, 512),
                                      in_=self.d_cos[:, c * 512:(c + 1) * 512]), 16)
            self.cs_need[slot] = s["sCS"].inc(
                SY.dma_start(out=self.Sf('cs', slot * 1024 + 512, 512),
                             in_=self.d_sin[:, c * 512:(c + 1) * 512]), 16)
            if not wkv_waited:
                T.wait_ge(s["sWKV"].h, 16)
                T.wait_ge(s["sCN"].h, s["sCN"].n)
                T.wait_ge(s["sCNG"].h, s["sCNG"].n)
                V.wait_ge(s["sCN"].h, s["sCN"].n)
                SC.wait_ge(s["sCN"].h, s["sCN"].n)
                wkv_waited = True
            T.wait_ge(s["sXT"].h, xt_n)
            for u in range(4):
                bank = u % 2
                lhs = [self.A('wkv', (u * 16 + f) * 128, 128) for f in range(16)]
                rhs = [xt_tile(f, slot) for f in range(16)]
                proj_accum(bank, lhs, rhs)
                kdst = self.A('kt', u * 4096 + c * 512, 512)
                self.rope_evac(kdst, self.bk[:, u:u + 1], bank, slot)
                pj_evac[bank] = (s["sKE"], s["sKE"].n)
            for half in range(2):
                wvsrc = self.d_wvt.rearrange("(f p) (u c) -> p f u c", p=128, c=128)[
                    :, half * 8:(half + 1) * 8]
                wvdst = self.A('wvh').rearrange("p (f u c) -> p f u c", f=8, u=4)
                G.wait_ge(s["sPJ"].h, self.wv_used)
                wv_n = s["sWKV"].inc(G.dma_start(out=wvdst, in_=wvsrc), 16)
                T.wait_ge(s["sWKV"].h, wv_n)
                wv_rr = self.A('wvh').rearrange("p (f u c) -> p f u c", f=8, u=4)
                for tt in range(4):
                    bank = 2 + tt
                    if half == 0 and pj_evac.get(bank) is not None:
                        sem, val = pj_evac[bank]
                        T.wait_ge(sem.h, val)
                    for f8 in range(8):
                        f = half * 8 + f8
                        last = (half == 1) and (f8 == 7)
                        ins = T.matmul(self.psum(bank), xt_tile(f, slot, tt * 128, 128),
                                       wv_rr[:, f8], start=(half == 0 and f8 == 0),
                                       stop=False)
                    if half == 1:
                        ins = T.matmul(self.psum(bank), self.C_('ones', 128, p0=0, p1=1),
                                       self.C_('bv', KVD, p0=0, p1=1),
                                       start=False, stop=True)
                    pj_n = s["sPJ"].inc(ins)
                self.wv_used = s["sPJ"].n
            for tt in range(4):
                bank = 2 + tt
                V.wait_ge(s["sPJ"].h, self.wv_used)
                ktile = c * 4 + tt
                vx_dst = self.A('vx').rearrange("p (a k e) -> p a k e", a=4, k=32)[:, :, ktile, 0:64]
                vy_dst = self.A('vy').rearrange("p (a k e) -> p a k e", a=4, k=32)[:, :, ktile, 64:128]
                ps_rr = self.psum(bank).rearrange("p (u e) -> p u e", u=4)
                V.tensor_copy(vx_dst, ps_rr[:, :, 0:64])
                ve_n = s["sVE"].inc(V.tensor_copy(vy_dst, ps_rr[:, :, 64:128]))
                pj_evac[bank] = (s["sVE"], ve_n)
            self.xt_free[slot] = s["sPJ"].n
            if c == 0:
                G.wait_ge(s["sCN"].h, s["sCN"].n)
                G.wait_ge(s["sCNG"].h, s["sCNG"].n)
                vx_rr = self.A('vx').rearrange("p (a k e) -> p a k e", a=4, k=32)
                vy_rr = self.A('vy').rearrange("p (a k e) -> p a k e", a=4, k=32)
                G.memset(vx_rr[:, :, :, 64:65], 1.0)
                G.memset(vy_rr[:, :, :, 32:33], 1.0)
                G.memset(vy_rr[:, :, :, 0:32], 0.0)
                s["sVE"].inc(G.memset(vy_rr[:, :, :, 33:64], 0.0))
            if c == 1:
                self.emit_q_stream(pj_evac, xt_tile)
                self.xt_free = [s["sPJ"].n, s["sPJ"].n]
                self.wv_used = s["sPJ"].n

    def emit_q_stream(self, pj_evac, xt_tile):
        nc, s = self.nc, self.s
        T, G = nc.tensor, nc.gpsimd
        G.wait_ge(s["sPJ"].h, s["sPJ"].n)
        for t in range(16):
            wslot = t % 2
            src = self.d_wqt.rearrange("(f p) (t2 c) -> p t2 f c", p=128, c=128)[:, t]
            dst = self.A('wqs', wslot * 2048, 2048).rearrange("p (f c) -> p f c", f=16)
            if t >= 2:
                G.wait_ge(s["sPJ"].h, self.wq_used[wslot])
            wq_n = s["sWQ"].inc(G.dma_start(out=dst, in_=src), 16)
            T.wait_ge(s["sWQ"].h, wq_n)
            for cc in range(2):
                bank = cc
                if pj_evac.get(bank) is not None:
                    sem, val = pj_evac[bank]
                    T.wait_ge(sem.h, val)
                for i in range(16):
                    ins = T.matmul(self.psum(bank),
                                   self.A('wqs', wslot * 2048 + i * 128, 128),
                                   xt_tile(i, cc), start=(i == 0), stop=(i == 15))
                s["sPJ"].inc(ins)
                qdst = self.A('qt', t * 1024 + cc * 512, 512)
                self.rope_evac(qdst, self.bq[:, t:t + 1], bank, cc)
                pj_evac[bank] = (s["sKE"], s["sKE"].n)
            self.wq_used[wslot] = s["sPJ"].n

    def _dump(self, pairs, waits):
        nc, s = self.nc, self.s
        SY = nc.sync
        for sem, val in waits:
            SY.wait_ge(sem.h, val)
        for dram, ap in pairs:
            s["sDBG"].inc(SY.dma_start(out=dram, in_=ap), 16)
        SY.wait_ge(s["sDBG"].h, s["sDBG"].n)

    def emit_dbg1(self):
        s = self.s
        self._dump([(self.d_dbg_kt, self.A('kt')), (self.d_dbg_vx, self.A('vx')),
                    (self.d_dbg_vy, self.A('vy')), (self.d_dbg_qt, self.A('qt'))],
                   [(s["sRC"], s["sRC"].n), (s["sVE"], s["sVE"].n)])

    def emit_dbg2(self):
        s = self.s
        self._dump([(self.d_dbg_yt, self.A('yt'))], [(s["aDONE"], 3)])

    def emit_dbg3(self):
        s = self.s
        self._dump([(self.d_dbg_z, self.Af('zacc')), (self.d_dbg_x1, self.A('x1tok')),
                    (self.d_dbg_x1t, self.A('x1T')),
                    (self.d_dbg_ln, self.Sf('ln', 0, 16))],
                   [(s["sLN"], s["sLN"].n), (s["sX1E"], s["sX1E"].n)])

    def emit_dbg4(self):
        s = self.s
        self._dump([(self.d_dbg_z, self.Af('zacc')),
                    (self.d_dbg_ln, self.Sf('tv', 0, 16)),
                    (self.d_dbg_hid, self.A('hidden'))],
                   [(s["sZ2"], s["sZ2"].n)])

    def emit_p2(self):
        nc, s = self.nc, self.s
        T, SC, V, SY = nc.tensor, nc.scalar, nc.vector, nc.sync
        rc_total, ve_total = s["sRC"].n, s["sVE"].n
        T.wait_ge(s["sRC"].h, rc_total)
        T.wait_ge(s["sVE"].h, ve_total)
        SC.wait_ge(s["sRC"].h, rc_total)
        V.wait_ge(s["sRC"].h, rc_total)
        CW = 1024
        self.gchunks = []
        for dram, nr, ncol in ((self.d_w1t, D, FH), (self.d_w2t, D, FH),
                               (self.d_w3t, FH, D)):
            for r0 in range(0, nr, 128):
                for c0 in range(0, ncol, CW):
                    self.gchunks.append((dram, r0, c0, min(CW, ncol - c0)))
        sqz0 = s["sQZ"].n
        SY.wait_ge(s["sVE"].h, ve_total)
        SY.wait_ge(s["sPJ"].h, s["sPJ"].n)
        for k, (dram, r0, c0, cw) in enumerate(self.gchunks):
            if k >= 2:
                SY.wait_ge(s["sQZ"].h, sqz0 + k - 1)
            s["sW12"].inc(SY.dma_start(out=self.Af('gma', (k % 2) * 1024, cw),
                                       in_=dram[r0:r0 + 128, c0:c0 + cw]), 16)
        snap0 = {k: sem.n for k, sem in self.s.items()}
        delta = None
        for name, eng in (('T', T), ('SC', SC), ('V', V)):
            with eng.register("j" + name) as reg:
                pid = nc.partition_id([eng.engine])
                eng.reg_mov(reg, pid)
                eng.reg_alu(reg, reg, 3, ALU.bitwise_and)
                for j in range(4):
                    for k, sem in self.s.items():
                        sem.n = snap0[k]
                    with eng.If_eq(reg, j):
                        self.emit_attention_body(j, name)
                    if delta is None:
                        delta = {k: sem.n - snap0[k] for k, sem in self.s.items()}
                eng.nop()
        for k, sem in self.s.items():
            sem.n = snap0[k] + delta[k]
        self.s["aDONE"].n = 3

    def emit_attention_body(self, j, active):
        nc, s = self.nc, self.s
        T0, SC0, V0 = nc.tensor, nc.scalar, nc.vector

        class _Null:
            def __getattr__(self, name):
                return lambda *a, **k: None

        NULL = _Null()
        T = T0 if active == 'T' else NULL
        SC = SC0 if active == 'SC' else NULL
        V = V0 if active == 'V' else NULL
        qk = {0: Sem(s["aQKx"].h), 1: Sem(s["aQKy"].h)}
        ex = {0: Sem(s["aEXx"].h), 1: Sem(s["aEXy"].h)}
        av = {0: Sem(s["aAVx"].h), 1: Sem(s["aAVy"].h)}
        mk, dn, bc, nm = (Sem(s["aMK"].h), Sem(s["aDN"].h),
                          Sem(s["aBC"].h), Sem(s["aNM"].h))
        perm = perm_blocks(j)
        bpos_blk = {blk: i for i, blk in enumerate(perm)}

        def bpos(kt_log):
            return bpos_blk[kt_log // 4] * 4 + kt_log % 4

        SCB = {0: 0, 1: 2}
        AVB = {0: 4, 1: 6}
        GSZ = 2
        neg12 = self.cf[:, 128:129]
        exp_at = {0: 0, 1: 0}
        nm_at = {0: [0, 0], 1: [0, 0]}
        slot_need = {0: [0, 0], 1: [0, 0]}
        tile_i = 0
        pending = None
        g_total = len(self.gchunks)
        units_total = 16 * 36
        g_done = [0]
        u_done = [0]
        sw12_0 = s["sW12"].n - 16 * g_total

        def gamma_abs(nu):
            u_done[0] += nu
            want = min(g_total, int(g_total * u_done[0] / units_total))
            while g_done[0] < want:
                k = g_done[0]
                dram, r0, c0, cw = self.gchunks[k]
                buf = self.Af('gma', (k % 2) * 1024, cw)
                SC.wait_ge(s["sW12"].h, sw12_0 + 16 * (k + 1))
                dummy = self.S_('sig', 0, cw)
                s["sQZ"].inc(SC.activation(dummy, buf, AF.Abs,
                                           accum_out=self.Sf('gbuf', k, 1)))
                g_done[0] += 1

        def epilogue_part2(p):
            t, qb, pav, dn_need = p
            for h in (0, 1):
                avb = AVB[h] + pav
                drow = 64 if h == 0 else 32
                yp0, yp1 = (0, 64) if h == 0 else (64, 128)
                denb = self.S_('kpre', 0, 512, p0=drow, p1=drow + 1).tensor if False else \
                    self.arn[drow:drow + 1, self.R['scr'][0] + 1024: self.R['scr'][0] + 1536]
                ones = self.C_('ones', 64, p0=drow, p1=drow + 1)
                T.wait_ge(dn.h, dn_need[h])
                if h == 0:
                    ins = T.matmul(self.psum(avb, p0=0, p1=64), ones, denb,
                                   start=True, stop=True)
                else:
                    ins = T.matmul(self.psum(avb, p0=64, p1=128), ones, denb,
                                   start=True, stop=True, tile_position=(32, 64))
                bc_n = bc.inc(ins)
                V.wait_ge(bc.h, bc_n)
                ysl = self.A('yt', t * 1024 + qb * 512, 512, p0=yp0, p1=yp1)
                ysc = self.S_('stt', 0, 512, p0=yp0, p1=yp1)
                nm.inc(V.tensor_tensor(ysl, ysc, self.psum(avb, p0=yp0, p1=yp1),
                                       op=ALU.mult))
                nm_at[h][pav] = nm.n

        for t in range(16):
            a = t // 4
            for qb in range(2):
                Lq = perm[qb]
                nkt = 4 * Lq + 4
                groups = [list(range(nkt))[i:i + GSZ] for i in range(0, nkt, GSZ)]
                p_av = tile_i % 2
                tile_i += 1
                first_av = {0: True, 1: True}
                av_bank_waited = False
                for gi, grp in enumerate(groups):
                    gp = gi % 2
                    for h in (0, 1):
                        T.wait_ge(ex[h].h, exp_at[h])
                    if not av_bank_waited:
                        for h in (0, 1):
                            T.wait_ge(nm.h, nm_at[h][p_av])
                        av_bank_waited = True
                    if gi == min(1, len(groups) - 1) and pending is not None:
                        epilogue_part2(pending)
                        pending = None
                    qk_ins = {}
                    for i, kt in enumerate(grp):
                        kb_ = bpos(kt)
                        for h in (0, 1):
                            p0, p1 = (0, 64) if h == 0 else (64, 128)
                            scb = SCB[h] + i
                            lhsT = self.A('kt', a * 4096 + kb_ * 128, 128, p0=p0, p1=p1)
                            rhs = self.A('qt', t * 1024 + qb * 512, 512, p0=p0, p1=p1)
                            qk_ins[h] = T.matmul(self.psum(scb), lhsT, rhs,
                                                 start=True, stop=True)
                    for h in (0, 1):
                        qk[h].inc(qk_ins[h])
                    for h in (0, 1):
                        SC.wait_ge(qk[h].h, qk[h].n)
                        if slot_need[h][gp]:
                            SC.wait_ge(av[h].h, slot_need[h][gp])
                        ebase = h * 2048 + gp * 1024
                        eout = self.A('expx', ebase, 1024)
                        ex_n = ex[h].inc(SC.activation(
                            eout, self.psum(SCB[h], n_f32=1024),
                            AF.Exp, bias=neg12, scale=0.125))
                        exp_at[h] = ex_n
                        mk_n0 = mk.n
                        for i, kt in enumerate(grp):
                            r = kt - 4 * Lq
                            if r >= 0:
                                V.wait_ge(ex[h].h, ex_n)
                                mrow = self.C_('masks', 512, off=r * 512)
                                esl = self.A('expx', ebase + i * 512, 512)
                                mk.inc(V.tensor_tensor(esl, esl, mrow, op=ALU.mult))
                        T.wait_ge(ex[h].h, ex_n)
                        if mk.n > mk_n0:
                            T.wait_ge(mk.h, mk.n)
                        for i, kt in enumerate(grp):
                            kb_ = bpos(kt)
                            if h == 0:
                                lhsT = self.A('vx', a * 2112 + kb_ * 66, 65)
                                avout = self.psum(AVB[h] + p_av, p0=0, p1=65)
                            else:
                                lhsT = self.A('vy', a * 4096 + kb_ * 128, 128)
                                avout = self.psum(AVB[h] + p_av)
                            rhs = self.A('expx', ebase + i * 512, 512)
                            ins = T.matmul(avout, lhsT, rhs,
                                           start=first_av[h],
                                           stop=(gi == len(groups) - 1 and i == len(grp) - 1))
                            first_av[h] = False
                        av_n = av[h].inc(ins)
                        slot_need[h][gp] = av_n
                gamma_abs(nkt)
                dn_need = {}
                V.wait_ge(bc.h, bc.n)
                for h in (0, 1):
                    avb = AVB[h] + p_av
                    drow = 64 if h == 0 else 32
                    yp0, yp1 = (0, 64) if h == 0 else (64, 128)
                    denf = self.Sf('kpre', 0, 512, p0=drow, p1=drow + 1)
                    dsc = self.Sf('m12', 0, 512, p0=drow, p1=drow + 1)
                    denb = self.arn[drow:drow + 1,
                                    self.R['scr'][0] + 1024: self.R['scr'][0] + 1536]
                    ysc = self.S_('stt', 0, 512, p0=yp0, p1=yp1)
                    V.wait_ge(av[h].h, av[h].n)
                    V.tensor_copy(ysc, self.psum(avb, p0=yp0, p1=yp1))
                    V.tensor_copy(denf, self.psum(avb, p0=drow, p1=drow + 1))
                    V.reciprocal(denf, denf)
                    dn_need[h] = dn.inc(V.tensor_copy(denb, denf))
                pending = (t, qb, p_av, dn_need)
        gamma_abs(units_total)
        epilogue_part2(pending)
        if active == 'T':
            T.wait_ge(nm.h, nm.n)
            T.matmul(self.psum(0, n_f32=1, p0=0, p1=1), self.C_('ones', 1, p0=0, p1=1),
                     self.C_('ones', 1, p0=0, p1=1), start=True, stop=True,
                     skip_group_check=True).then_inc(s["aDONE"].h, 1)
        elif active == 'SC':
            SC.activation(self.S_('stt', 512, 1, p0=0, p1=1),
                          self.S_('stt', 512, 1, p0=0, p1=1), AF.Identity,
                          bias=0.0, scale=1.0).then_inc(s["aDONE"].h, 1)
        else:
            V.tensor_copy(self.S_('stt', 514, 1, p0=0, p1=1),
                          self.S_('stt', 512, 1, p0=0, p1=1)).then_inc(s["aDONE"].h, 1)

    def emit_p3(self):
        nc, s = self.nc, self.s
        T, SC, V, SY, G = nc.tensor, nc.scalar, nc.vector, nc.sync, nc.gpsimd
        SY.wait_ge(s["aDONE"].h, 3)
        zdst = self.Af('zacc').rearrange("p (ss c) -> p ss c", ss=8)
        zsrc = self.d_xo.rearrange("(ss p) c -> p ss c", p=128)
        zi_n = s["sZI"].inc(SY.dma_start(out=zdst, in_=zsrc), 16)

        G.wait_ge(s["aDONE"].h, 3)
        op_evac = {}
        for oc in range(4):
            wsrc = self.d_wot.rearrange("(f p) c -> p f c", p=128)[:, :, oc * 512:(oc + 1) * 512]
            wdst = self.A('woT').rearrange("p (f c) -> p f c", f=16)
            if oc > 0:
                G.wait_ge(s["sOP"].h, s["sOP"].n)
            wo_n = s["sWO"].inc(G.dma_start(out=wdst, in_=wsrc), 16)
            T.wait_ge(s["sWO"].h, wo_n)
            if oc == 0:
                T.wait_ge(s["aDONE"].h, 3)
            for sidx in range(8):
                bank = sidx % 2
                if op_evac.get(bank) is not None:
                    T.wait_ge(s["sZA"].h, op_evac[bank])
                for tt in range(16):
                    T.matmul(self.psum(bank), self.A('yt', tt * 1024 + sidx * 128, 128),
                             self.A('woT', tt * 512, 512), start=(tt == 0), stop=False)
                ins = T.matmul(self.psum(bank), self.C_('ones', 128, p0=0, p1=1),
                               self.C_('bo', 512, off=oc * 512, p0=0, p1=1),
                               start=False, stop=True)
                op_n = s["sOP"].inc(ins)
                V.wait_ge(s["sOP"].h, op_n)
                V.wait_ge(s["sZI"].h, zi_n)
                zsl = self.Af('zacc', sidx * 2048 + oc * 512, 512)
                za_n = s["sZA"].inc(V.tensor_tensor(zsl, zsl, self.psum(bank), op=ALU.add))
                op_evac[bank] = za_n

        V.wait_ge(s["sZA"].h, s["sZA"].n)
        x1e = {}
        for sidx in range(8):
            self.emit_ln(self.Af('zacc', sidx * 2048, 2048),
                         self.A('x1tok', sidx * 2048, 2048), s["sLN"])
            ln_n = s["sLN"].n
            T.wait_ge(s["sLN"].h, ln_n)
            for half in range(2):
                bank = 2 + half
                if x1e.get(bank) is not None:
                    T.wait_ge(s["sX1E"].h, x1e[bank])
                for f8 in range(8):
                    f = half * 8 + f8
                    ins = T.transpose(self.psum_bf(bank, 128, off_bf=f8 * 128),
                                      self.A('x1tok', sidx * 2048 + f * 128, 128),
                                      self.C_('ident', 128))
                op_n = s["sOP"].inc(ins)
                V.wait_ge(s["sOP"].h, op_n)
                dst = self.A('x1T').rearrange("p (f c) -> p f c", f=16)[
                    :, half * 8:(half + 1) * 8, sidx * 128:(sidx + 1) * 128]
                src = self.psum_bf(bank, 1024).rearrange("p (f c) -> p f c", f=8)
                x1e[bank] = s["sX1E"].inc(V.tensor_copy(dst, src))

    def emit_ln(self, src_f32, out_ap, sem):
        nc = self.nc
        SC, V = nc.scalar, nc.vector
        n_free = src_f32.shape[-1]
        sum_ = self.Sf('ln', 0, 1)
        nmean = self.Sf('ln', 1, 1)
        var = self.Sf('ln', 2, 1)
        m2 = self.Sf('ln', 3, 1)
        rstd = self.Sf('ln', 4, 1)
        sq = self.Sf('ln', 5, 1)
        dummy = self.Af('lnd', 0, n_free)
        V.tensor_reduce(sum_, src_f32, axis=AX.X, op=ALU.add)
        V.scalar_tensor_tensor(dummy, src_f32, 1.0, src_f32, op0=ALU.mult,
                               op1=ALU.mult, accum_out=sq)
        n1 = sem.inc(V.tensor_scalar(nmean, sum_, -1.0 / n_free, None, op0=ALU.mult))
        V.wait_ge(sem.h, n1)
        n2 = sem.inc(V.tensor_scalar(m2, nmean, nmean, None, op0=ALU.mult))
        V.wait_ge(sem.h, n2)
        n2b = sem.inc(V.tensor_scalar(var, sq, 1.0 / n_free, m2,
                                      op0=ALU.mult, op1=ALU.subtract))
        V.wait_ge(sem.h, n2b)
        vn = sem.inc(V.tensor_scalar(var, var, EPS_LN, None, op0=ALU.add))
        SC.wait_ge(sem.h, vn)
        sn = sem.inc(SC.activation(rstd, var, AF.Sqrt))
        V.wait_ge(sem.h, sn)
        n3 = sem.inc(V.reciprocal(rstd, rstd))
        V.wait_ge(sem.h, n3)
        sem.inc(V.tensor_scalar(out_ap, src_f32, nmean, rstd, op0=ALU.add, op1=ALU.mult))

    def emit_p4(self):
        nc, s = self.nc, self.s
        T, SC, V, SY = nc.tensor, nc.scalar, nc.vector, nc.sync
        ln_total = s["sLN"].n
        SY.wait_ge(s["sLN"].h, ln_total)
        SC.wait_ge(s["sLN"].h, ln_total)
        SC.wait_ge(s["sX1E"].h, s["sX1E"].n)

        CW_G = 1024
        gs = []
        pos = 0
        for nr, ncol in ((D, FH), (D, FH), (FH, D)):
            cnt = (nr // 128) * ((ncol + CW_G - 1) // CW_G)
            gs.append(list(range(pos, pos + cnt)))
            pos += cnt
        V.wait_ge(s["sQZ"].h, s["sQZ"].n)
        qz_n = None
        for mi in range(3):
            qz_n = s["sQZ"].inc(V.tensor_reduce(
                self.Sf('ln', 5 + mi, 1),
                self.Sf('gbuf', gs[mi][0], len(gs[mi])), axis=AX.X, op=ALU.add))
        V.wait_ge(s["sQZ"].h, qz_n)
        T.wait_ge(s["sQZ"].h, qz_n)
        fp_n = s["sFP"].inc(T.matmul(self.psum(0, n_f32=3, p0=0, p1=1), self.cf[:, 0:1],
                                     self.Sf('ln', 5, 3), start=True, stop=True))
        V.wait_ge(s["sFP"].h, fp_n)
        NW = float(D) * float(FH)
        trow = self.Sf('tv', 6, 3, p0=0, p1=1)
        qz_n = s["sQZ"].inc(V.tensor_scalar(trow, self.psum(0, n_f32=3, p0=0, p1=1),
                                            0.5 / NW, 0.5 * EPS_Q, op0=ALU.mult, op1=ALU.add))
        T.wait_ge(s["sQZ"].h, qz_n)
        fp_n = s["sFP"].inc(T.matmul(self.psum(1, n_f32=3), self.cf[0:1, 0:128],
                                     trow, start=True, stop=True))
        V.wait_ge(s["sFP"].h, fp_n)
        tvec = self.Sf('tv', 0, 3)
        ntvec = self.Sf('tv', 3, 3)
        qz_n = s["sQZ"].inc(V.tensor_copy(tvec, self.psum(1, n_f32=3)))
        V.wait_ge(s["sQZ"].h, qz_n)
        qz_n = s["sQZ"].inc(V.tensor_scalar(ntvec, tvec, -1.0, None, op0=ALU.mult))
        tv_ready = qz_n

        def quantize(dst_bf, src_f32, mi):
            ncols = src_f32.shape[-1]
            last = None
            for c0 in range(0, ncols, 512):
                cw = min(512, ncols - c0)
                Bp = self.psum(6, n_f32=cw)
                V.tensor_scalar(Bp, src_f32[:, c0:c0 + cw],
                                self.Sf('tv', 3 + mi, 1), None, op0=ALU.is_lt)
                last = V.scalar_tensor_tensor(
                    dst_bf[:, c0:c0 + cw], src_f32[:, c0:c0 + cw],
                    self.Sf('tv', mi, 1), Bp, op0=ALU.is_gt, op1=ALU.subtract)
            return s["sQZ"].inc(last)

        V.wait_ge(s["sQZ"].h, tv_ready)
        zi2 = None
        for sidx in range(8):
            zi2 = s["sZ2"].inc(V.tensor_copy(self.Af('zacc', sidx * 2048, 2048),
                                             self.A('x1tok', sidx * 2048, 2048)))
        group_sizes = [8, 8, 8, 8, 8, 4]
        gstart = 0
        w12_dma_gate = 0
        hp_evac = {}
        fp_evac = {}
        wq_mm_done = [0, 0]
        hid_used = 0
        self.w3_qz_at = [0, 0]
        self.w3_mm_at = [0, 0]
        self.sig_used = [0, 0]

        def w12q_slot(h):
            if h % 2 == 0:
                return (self.A('w12q', 0, 2048), self.A('w12q', 2048, 2048))
            return (self.S_('w12qB', 0, 2048), self.S_('w12qB', 2048, 2048))

        for g, gsz in enumerate(group_sizes):
            for hs in range(gsz):
                h = gstart + hs
                if w12_dma_gate:
                    SY.wait_ge(s["sQZ"].h, w12_dma_gate)
                n1 = s["sW12"].inc(SY.dma_start(
                    out=self.Af('w12f', 0, 2048).rearrange("p (f c) -> p f c", f=16),
                    in_=self.d_w1t.rearrange("(f p) m -> p f m", p=128)[:, :, h * 128:(h + 1) * 128]), 16)
                n2 = s["sW12"].inc(SY.dma_start(
                    out=self.Af('w12f', 2048, 2048).rearrange("p (f c) -> p f c", f=16),
                    in_=self.d_w2t.rearrange("(f p) m -> p f m", p=128)[:, :, h * 128:(h + 1) * 128]), 16)
                V.wait_ge(s["sW12"].h, n2)
                V.wait_ge(s["sQZ"].h, tv_ready)
                if wq_mm_done[h % 2]:
                    V.wait_ge(s["sHP"].h, wq_mm_done[h % 2])
                if self.w3_mm_at[h % 2]:
                    V.wait_ge(s["sFP"].h, self.w3_mm_at[h % 2])
                dstA, dstB = w12q_slot(h)
                qz_w1 = quantize(dstA, self.Af('w12f', 0, 2048), 0)
                qz_after = quantize(dstB, self.Af('w12f', 2048, 2048), 1)
                w12_dma_gate = qz_after
                hp_done = {}
                for mi in range(2):
                    src = dstA if mi == 0 else dstB
                    T.wait_ge(s["sQZ"].h, qz_w1 if mi == 0 else qz_after)
                    for cc in range(2):
                        bank = mi * 2 + cc
                        if hp_evac.get(bank) is not None:
                            sem, val = hp_evac[bank]
                            T.wait_ge(sem.h, val)
                        for i in range(16):
                            ins = T.matmul(self.psum(bank),
                                           src[:, i * 128:(i + 1) * 128],
                                           self.A('x1T', i * 1024 + cc * 512, 512),
                                           start=(i == 0), stop=(i == 15))
                        hp_done[bank] = s["sHP"].inc(ins)
                wq_mm_done[h % 2] = s["sHP"].n
                for cc in range(2):
                    b1, b2 = cc, 2 + cc
                    sig = self.Sf('sig', 0, 512) if cc == 0 else self.Sf('m12', 0, 512)
                    SC.wait_ge(s["sHP"].h, hp_done[b1])
                    SC.wait_ge(s["sHD"].h, self.sig_used[cc])
                    if hid_used:
                        SC.wait_ge(s["sFP"].h, hid_used)
                    sg_n = s["sSG"].inc(SC.activation(sig, self.psum(b1), AF.Sigmoid))
                    V.wait_ge(s["sSG"].h, sg_n)
                    V.wait_ge(s["sHP"].h, hp_done[b2])
                    hd_n = s["sHD"].inc(V.tensor_tensor(
                        self.A('hidden', hs * 1024 + cc * 512, 512),
                        sig, self.psum(b2), op=ALU.mult))
                    self.sig_used[cc] = hd_n
                    hp_evac[b1] = (s["sSG"], sg_n)
                    hp_evac[b2] = (s["sHD"], hd_n)
            hd_total = s["sHD"].n
            T.wait_ge(s["sHD"].h, hd_total)
            for oc in range(4):
                w3f = self.Af('w12f', 0, gsz * 512)
                w3q = self.A('w12q', 0, gsz * 512)
                SY.wait_ge(s["sZ2"].h, zi2)
                SY.wait_ge(s["sQZ"].h, s["sQZ"].n)
                src = self.d_w3t[gstart * 128:(gstart + gsz) * 128,
                                 oc * 512:(oc + 1) * 512]
                w3_n = s["sW3"].inc(SY.dma_start(
                    out=w3f.rearrange("p (hs c) -> p hs c", hs=gsz),
                    in_=src.rearrange("(hs p) c -> p hs c", p=128)), 16)
                V.wait_ge(s["sW3"].h, w3_n)
                if self.w3_mm_at[0]:
                    V.wait_ge(s["sFP"].h, self.w3_mm_at[0])
                V.wait_ge(s["sHP"].h, wq_mm_done[0])
                q3_total = quantize(w3q, w3f, 2)
                T.wait_ge(s["sQZ"].h, q3_total)
                for sidx in range(8):
                    bank = (4, 5, 7)[sidx % 3]
                    if fp_evac.get(bank) is not None:
                        T.wait_ge(s["sZ2"].h, fp_evac[bank])
                    for hs in range(gsz):
                        ins = T.matmul(self.psum(bank),
                                       self.A('hidden', hs * 1024 + sidx * 128, 128),
                                       w3q[:, hs * 512:(hs + 1) * 512],
                                       start=(hs == 0), stop=(hs == gsz - 1))
                    fp_n = s["sFP"].inc(ins)
                    V.wait_ge(s["sFP"].h, fp_n)
                    zsl = self.Af('zacc', sidx * 2048 + oc * 512, 512)
                    z2_n = s["sZ2"].inc(V.tensor_tensor(
                        zsl, zsl, self.psum(bank), op=ALU.add))
                    fp_evac[bank] = z2_n
                self.w3_mm_at[0] = s["sFP"].n
            hid_used = s["sFP"].n
            w12_dma_gate = s["sQZ"].n
            gstart += gsz

    def emit_p5(self):
        nc, s = self.nc, self.s
        SY, V = nc.sync, nc.vector
        z2_total = s["sZ2"].n
        V.wait_ge(s["sZ2"].h, z2_total)
        out_used = [0, 0]
        for sidx in range(8):
            slot = sidx % 2
            obuf = self.Af('outs', 0, 2048) if slot == 0 else self.Af('w1r', 2048, 2048)
            if out_used[slot]:
                V.wait_ge(s["sOUT"].h, out_used[slot])
            self.emit_ln(self.Af('zacc', sidx * 2048, 2048), obuf, s["sL2"])
            l2_n = s["sL2"].n
            SY.wait_ge(s["sL2"].h, l2_n)
            ins = SY.dma_start(out=self.d_out[sidx * 128:(sidx + 1) * 128, :], in_=obuf)
            out_used[slot] = s["sOUT"].inc(ins, 16)
        SY.wait_ge(s["sOUT"].h, s["sOUT"].n)


_BUILD_CACHE = {}


def _get_program(dbg=0):
    if dbg not in _BUILD_CACHE:
        kb = KB(dbg)
        _BUILD_CACHE[dbg] = (kb.build(), kb)
    return _BUILD_CACHE[dbg]


def make_consts():
    ident = np.eye(128, dtype=ml_dtypes.bfloat16)
    P = np.zeros((128, 128), dtype=ml_dtypes.bfloat16)
    for base in (0, 64):
        for dd in range(32):
            P[base + dd + 32, base + dd] = 1
            P[base + dd, base + dd + 32] = 1
    masks = np.zeros((128, 4 * 512), dtype=ml_dtypes.bfloat16)
    p = np.arange(128)[:, None]
    q = np.arange(512)[None, :]
    for r in range(4):
        masks[:, r * 512:(r + 1) * 512] = (q >= 128 * r + p).astype(ml_dtypes.bfloat16)
    ones = np.ones((128, 128), dtype=ml_dtypes.bfloat16)
    cbf = np.ascontiguousarray(np.concatenate([ident, P, masks, ones], axis=1).astype(np.float32))
    cf32 = np.zeros((128, 132), dtype=np.float32)
    cf32[:, 0:128] = 1.0
    cf32[:, 128] = -12.0
    return cbf, cf32


def make_core_inputs(inputs, cid):
    x = np.asarray(inputs["L0_attn_x"])
    batch, j = cid // 4, cid % 4
    perm = perm_blocks(j)
    xb = x[batch]
    x_perm = np.concatenate([xb[b * BLK:(b + 1) * BLK] for b in perm], axis=0)
    xt = np.ascontiguousarray(x_perm.T.astype(np.float32))
    xo = np.ascontiguousarray(x_perm[:NTOK].astype(np.float32))
    jj = np.arange(32, dtype=np.float64)
    theta = 10000.0 ** (-2.0 * jj / 64.0)
    tok = np.concatenate([np.arange(b * BLK, (b + 1) * BLK) for b in perm])
    ang = (tok[None, :] + 1.0) * theta[np.arange(128) % 32][:, None]
    cos_t = np.cos(ang).astype(np.float32)
    sgn = np.where((np.arange(128) % 64) < 32, -1.0, 1.0)[:, None]
    sin_t = (np.sin(ang) * sgn).astype(np.float32)
    return dict(xt=xt, xo=xo, cos_t=np.ascontiguousarray(cos_t),
                sin_t=np.ascontiguousarray(sin_t))


def make_shared_inputs(inputs):
    qp = qrowperm()
    gi = lambda k: np.asarray(inputs[k])
    d = dict(
        wqt=np.ascontiguousarray(gi("L0_attn_Wq")[qp].T.astype(np.float32)),
        wkt=np.ascontiguousarray(gi("L0_attn_Wk").T.astype(np.float32)),
        wvt=np.ascontiguousarray(gi("L0_attn_Wv").T.astype(np.float32)),
        wot=np.ascontiguousarray(gi("L0_attn_Wo")[:, qp].T.astype(np.float32)),
        w1t=np.ascontiguousarray(gi("L0_ffn_w1").T.astype(np.float32)),
        w2t=np.ascontiguousarray(gi("L0_ffn_w2").T.astype(np.float32)),
        w3t=np.ascontiguousarray(gi("L0_ffn_w3").T.astype(np.float32)),
        bq_t=np.ascontiguousarray(gi("L0_attn_bq")[qp].reshape(16, 128).T.astype(np.float32)),
        bk_t=np.ascontiguousarray(gi("L0_attn_bk").reshape(4, 128).T.astype(np.float32)),
        bo_r=np.ascontiguousarray(gi("L0_attn_bo").reshape(1, D).astype(np.float32)),
        bv_r=np.ascontiguousarray(gi("L0_attn_bv").reshape(1, KVD).astype(np.float32)),
    )
    cbf, cf32 = make_consts()
    d["consts_bf"] = cbf
    d["cf32"] = cf32
    return d


def kernel(**inputs):
    nc, kb = _get_program(0)
    shared = make_shared_inputs(inputs)
    in_maps = []
    for cid in range(8):
        m = dict(shared)
        m.update(make_core_inputs(inputs, cid))
        in_maps.append(m)
    res = run_bass_kernel_spmd(nc, in_maps, core_ids=list(range(8)))
    out = np.zeros((B, T, D), dtype=np.float32)
    for cid in range(8):
        batch, j = cid // 4, cid % 4
        o = res.results[cid]["out"]
        out[batch, j * BLK:(j + 1) * BLK] = o[:BLK]
        out[batch, (7 - j) * BLK:(8 - j) * BLK] = o[BLK:]
    return out


# version 15 (final)
# speedup vs baseline: 1.0123x; 1.0123x over previous
import numpy as np
import ml_dtypes

import concourse.bass as bass
import concourse.mybir as mybir
from concourse.bass_utils import run_bass_kernel_spmd

F32 = mybir.dt.float32
BF16 = mybir.dt.bfloat16
AF = mybir.ActivationFunctionType
ALU = mybir.AluOpType
AX = mybir.AxisListType

B, T, D = 2, 4096, 2048
KVD = 512
FH = 5632
NBLK = 8
BLK = 512
NTOK = 1024
EPS_LN = 1e-5
EPS_Q = 1e-8


def perm_blocks(j):
    return [j, 7 - j] + [b for b in range(NBLK) if b not in (j, 7 - j)]


def qrowperm():
    idx = []
    for t in range(16):
        a, g = t // 4, t % 4
        for hh in (2 * a, 2 * a + 1):
            base = hh * 256 + g * 64
            idx.extend(range(base, base + 64))
    return np.array(idx, dtype=np.int64)


class Sem:
    def __init__(self, h):
        self.h = h
        self.n = 0

    def inc(self, instr, k=1):
        if instr is not None:
            instr.then_inc(self.h, k)
        self.n += k
        return self.n


class KB:
    NELS = 101376
    R = dict(
        kt=(0, 16384),
        vx=(16384, 8448),
        vy=(24832, 16384),
        qt=(41216, 16384),
        xT=(57600, 16384),
        yt=(57600, 16384),
        x1T=(57600, 16384),
        wkv=(73984, 8192),
        wvh=(82176, 4096),
        expx=(73984, 4096),
        woT=(73984, 8192),
        lnd=(73984, 4096),
        w1r=(73984, 8192),
        w12f=(73984, 8192),
        wqs=(82176, 4096),
        w12q=(82176, 4096),
        gma=(82176, 4096),
        consts=(86272, 5248),
        zacc=(0, 32768),
        x1tok=(32768, 16384),
        hidden=(49152, 8192),
        scr=(91520, 9856),
        outs=(49152, 4096),
    )
    SCR = dict(
        w12qB=(0, 4096),
        kpre=(0, 1024),
        cs=(1024, 4096),
        ln=(5120, 64),
        m12=(5184, 2048),
        sig=(7232, 1024),
        gbuf=(8256, 640),
        tv=(8896, 64),
        stt=(8960, 640),
        w3f=(9600, 256),
    )
    CO = dict(ident=0, P=128, masks=256, ones=2304, bo=2432, bv=4480)

    def __init__(self, dbg_phase=0):
        self.dbg = dbg_phase
        nc = bass.Bass("TRN2", target_bir_lowering=False)
        self.nc = nc
        self.d_xt = nc.dram_tensor("xt", [D, T], F32, kind="ExternalInput").ap()
        self.d_xo = nc.dram_tensor("xo", [NTOK, D], F32, kind="ExternalInput").ap()
        self.d_cos = nc.dram_tensor("cos_t", [128, T], F32, kind="ExternalInput").ap()
        self.d_sin = nc.dram_tensor("sin_t", [128, T], F32, kind="ExternalInput").ap()
        self.d_wqt = nc.dram_tensor("wqt", [D, D], F32, kind="ExternalInput").ap()
        self.d_wkt = nc.dram_tensor("wkt", [D, KVD], F32, kind="ExternalInput").ap()
        self.d_wvt = nc.dram_tensor("wvt", [D, KVD], F32, kind="ExternalInput").ap()
        self.d_wot = nc.dram_tensor("wot", [D, D], F32, kind="ExternalInput").ap()
        self.d_w1t = nc.dram_tensor("w1t", [D, FH], F32, kind="ExternalInput").ap()
        self.d_w2t = nc.dram_tensor("w2t", [D, FH], F32, kind="ExternalInput").ap()
        self.d_w3t = nc.dram_tensor("w3t", [FH, D], F32, kind="ExternalInput").ap()
        self.d_bqt = nc.dram_tensor("bq_t", [128, 16], F32, kind="ExternalInput").ap()
        self.d_bkt = nc.dram_tensor("bk_t", [128, 4], F32, kind="ExternalInput").ap()
        self.d_bo = nc.dram_tensor("bo_r", [1, D], F32, kind="ExternalInput").ap()
        self.d_bv = nc.dram_tensor("bv_r", [1, KVD], F32, kind="ExternalInput").ap()
        self.d_cbf = nc.dram_tensor("consts_bf", [128, 2432], F32, kind="ExternalInput").ap()
        self.d_cf = nc.dram_tensor("cf32", [128, 132], F32, kind="ExternalInput").ap()
        self.d_out = nc.dram_tensor("out", [NTOK, D], F32, kind="ExternalOutput").ap()
        if self.dbg:
            dt_ = nc.dram_tensor
            self.d_dbg_kt = dt_("dbg_kt", [128, 16384], BF16, kind="ExternalOutput").ap()
            self.d_dbg_vx = dt_("dbg_vx", [128, 8448], BF16, kind="ExternalOutput").ap()
            self.d_dbg_vy = dt_("dbg_vy", [128, 16384], BF16, kind="ExternalOutput").ap()
            self.d_dbg_qt = dt_("dbg_qt", [128, 16384], BF16, kind="ExternalOutput").ap()
            self.d_dbg_yt = dt_("dbg_yt", [128, 16384], BF16, kind="ExternalOutput").ap()
            self.d_dbg_z = dt_("dbg_z", [128, 16384], F32, kind="ExternalOutput").ap()
            self.d_dbg_x1 = dt_("dbg_x1", [128, 16384], BF16, kind="ExternalOutput").ap()
            self.d_dbg_x1t = dt_("dbg_x1t", [128, 16384], BF16, kind="ExternalOutput").ap()
            self.d_dbg_ln = dt_("dbg_ln", [128, 16], F32, kind="ExternalOutput").ap()
            self.d_dbg_hid = dt_("dbg_hid", [128, 8192], BF16, kind="ExternalOutput").ap()

    def A(self, region, off=0, n=None, p0=0, p1=128):
        base, sz = self.R[region]
        if n is None:
            n = sz - off
        assert off + n <= sz, (region, off, n, sz)
        return self.arn[p0:p1, base + off: base + off + n]

    def Af(self, region, off_f32=0, n_f32=None, p0=0, p1=128):
        base, sz = self.R[region]
        if n_f32 is None:
            n_f32 = sz // 2 - off_f32
        assert 2 * (off_f32 + n_f32) <= sz, (region, off_f32, n_f32)
        ap = self.arn[p0:p1, base + 2 * off_f32: base + 2 * (off_f32 + n_f32)]
        return ap.bitcast(F32)

    def S_(self, key, off=0, n=None, p0=0, p1=128):
        base, sz = self.SCR[key]
        if n is None:
            n = sz - off
        assert off + n <= sz, (key, off, n)
        o = self.R['scr'][0] + base
        return self.arn[p0:p1, o + off: o + off + n]

    def Sf(self, key, off_f32=0, n_f32=1, p0=0, p1=128):
        base, sz = self.SCR[key]
        assert 2 * (off_f32 + n_f32) <= sz, (key, off_f32, n_f32)
        o = self.R['scr'][0] + base
        ap = self.arn[p0:p1, o + 2 * off_f32: o + 2 * (off_f32 + n_f32)]
        return ap.bitcast(F32)

    def C_(self, key, n, off=0, p0=0, p1=128):
        return self.A('consts', self.CO[key] + off, n, p0, p1)

    def psum(self, bank, n_f32=512, off_f32=0, p0=0, p1=128):
        return self.psa[p0:p1, bank * 512 + off_f32: bank * 512 + off_f32 + n_f32]

    def psum_bf(self, bank, n_bf=1024, off_bf=0, p0=0, p1=128):
        assert off_bf % 2 == 0 and (off_bf + n_bf) <= 1024
        ap = self.psa[p0:p1, bank * 512 + off_bf // 2: bank * 512 + (off_bf + n_bf) // 2]
        return ap.bitcast(BF16)

    def build(self):
        nc = self.nc
        from contextlib import ExitStack
        with ExitStack() as st:
            self.arn_t = st.enter_context(nc.sbuf_tensor("ARN", [128, self.NELS], BF16))
            self.arn = self.arn_t[:]
            self.psa_t = st.enter_context(nc.psum_tensor("PSA", [128, 4096], F32))
            self.psa = self.psa_t[:]
            self.cf = st.enter_context(nc.sbuf_tensor("CF", [128, 132], F32))[:]
            self.bq = st.enter_context(nc.sbuf_tensor("BQ", [128, 16], F32))[:]
            self.bk = st.enter_context(nc.sbuf_tensor("BK", [128, 4], F32))[:]

            sems = {}
            for name in ["sXT", "sWKV", "sWV", "sCS", "sWQ", "sCN", "sCNG",
                         "sPJ", "sKE", "sRP", "sRC", "sVE",
                         "aQKx", "aQKy", "aEXx", "aEXy", "aMK", "aAVx", "aAVy",
                         "aDN", "aBC", "aNM", "aDONE",
                         "sZI", "sWO", "sOP", "sZA", "sLN", "sX1E",
                         "sW12", "sQZ", "sHP", "sSG", "sHD",
                         "sW3", "sQ3", "sFP", "sZ2", "sL2", "sOUT", "sDBG"]:
                sems[name] = Sem(st.enter_context(nc.semaphore(name)))
            self.s = sems

            with nc.Block():
                self.emit_all()
        return nc

    def emit_all(self):
        self.emit_p0()
        self.emit_p1()
        if self.dbg == 1:
            self.emit_dbg1()
            return
        self.emit_p2()
        if self.dbg == 2:
            self.emit_dbg2()
            return
        self.emit_p3()
        if self.dbg == 3:
            self.emit_dbg3()
            return
        self.emit_p4()
        if self.dbg == 4:
            self.emit_dbg4()
            return
        self.emit_p5()

    def emit_p0(self):
        nc, s = self.nc, self.s
        SY, G = nc.sync, nc.gpsimd
        s["sCNG"].inc(G.dma_start(out=self.A('consts', 0, 2432), in_=self.d_cbf), 16)
        s["sCN"].inc(SY.dma_start(out=self.cf, in_=self.d_cf), 16)
        s["sCN"].inc(SY.dma_start(out=self.bq, in_=self.d_bqt), 16)
        s["sCN"].inc(SY.dma_start(out=self.bk, in_=self.d_bkt), 16)
        s["sCNG"].inc(G.dma_start(out=self.C_('bo', D, p0=0, p1=1), in_=self.d_bo), 16)
        s["sCNG"].inc(G.dma_start(out=self.C_('bv', KVD, p0=0, p1=1), in_=self.d_bv), 16)
        wk_src = self.d_wkt.rearrange("(f p) (u c) -> p u f c", p=128, c=128)
        wkdst = self.A('wkv', 0, 8192).rearrange("p (u f c) -> p u f c", u=4, f=16)
        s["sWKV"].inc(G.dma_start(out=wkdst, in_=wk_src), 16)

    def rope_evac(self, dst_ap, bias_col, pj_bank, cs_slot):
        nc, s = self.nc, self.s
        T, SC, V = nc.tensor, nc.scalar, nc.vector
        kslot = self.kslot_i % 2
        self.kslot_i += 1
        kpre = self.S_('kpre', kslot * 512, 512)
        if self.kpre_last[kslot]:
            SC.wait_ge(s["sRC"].h, self.kpre_last[kslot])
        SC.wait_ge(s["sPJ"].h, s["sPJ"].n)
        ke_n = s["sKE"].inc(SC.activation(kpre, self.psum(pj_bank), AF.Identity,
                                          bias=bias_col, scale=1.0))
        rbank = 6 + kslot
        T.wait_ge(s["sKE"].h, ke_n)
        if self.rp_last[kslot]:
            T.wait_ge(s["sRC"].h, self.rp_last[kslot])
        rp_n = s["sRP"].inc(T.matmul(self.psum(rbank), self.C_('P', 128),
                                     kpre, start=True, stop=True))
        V.wait_ge(s["sRP"].h, rp_n)
        V.wait_ge(s["sCS"].h, self.cs_need[cs_slot])
        m1 = self.Sf('m12', 0, 512)
        m2 = self.Sf('m12', 512, 512)
        cos = self.Sf('cs', cs_slot * 1024, 512)
        sin = self.Sf('cs', cs_slot * 1024 + 512, 512)
        V.tensor_tensor(m1, kpre, cos, op=ALU.mult)
        V.tensor_tensor(m2, self.psum(rbank), sin, op=ALU.mult)
        rc_n = s["sRC"].inc(V.tensor_tensor(dst_ap, m1, m2, op=ALU.add))
        self.rp_last[kslot] = rc_n
        self.kpre_last[kslot] = rc_n
        self.cs_last[cs_slot] = rc_n
        return rc_n

    def emit_p1(self):
        nc, s = self.nc, self.s
        T, SC, V, SY, G = nc.tensor, nc.scalar, nc.vector, nc.sync, nc.gpsimd
        self.rp_last = [0, 0]
        self.kpre_last = [0, 0]
        self.cs_last = [0, 0]
        self.cs_need = [0, 0]
        self.kslot_i = 0
        self.wq_used = [0, 0]
        self.wv_used = 0
        pj_evac = {}
        self.xt_free = [0, 0]
        xt_rr = self.A('xT').rearrange("p (f r) -> p f r", f=16)

        def xt_tile(f, slot, col0=0, ncol=512):
            base = self.R['xT'][0] + f * 1024 + slot * 512 + col0
            return self.arn[:, base: base + ncol]

        def proj_accum(bank, lhs, rhs, extra=None):
            if pj_evac.get(bank) is not None:
                sem, val = pj_evac[bank]
                T.wait_ge(sem.h, val)
            n = len(lhs)
            for i in range(n):
                last = (i == n - 1) and extra is None
                ins = T.matmul(self.psum(bank), lhs[i], rhs[i],
                               start=(i == 0), stop=last)
            if extra is not None:
                ins = T.matmul(self.psum(bank), extra[0], extra[1],
                               start=False, stop=True)
            return s["sPJ"].inc(ins)

        wkv_waited = False
        for c in range(8):
            slot = c % 2
            src = self.d_xt.rearrange("(f p) t -> p f t", p=128)[:, :, c * 512:(c + 1) * 512]
            dst = xt_rr[:, :, slot * 512:(slot + 1) * 512]
            G.wait_ge(s["sPJ"].h, self.xt_free[slot])
            xt_n = s["sXT"].inc(G.dma_start(out=dst, in_=src), 16)
            SY.wait_ge(s["sRC"].h, self.cs_last[slot])
            s["sCS"].inc(SY.dma_start(out=self.Sf('cs', slot * 1024, 512),
                                      in_=self.d_cos[:, c * 512:(c + 1) * 512]), 16)
            self.cs_need[slot] = s["sCS"].inc(
                SY.dma_start(out=self.Sf('cs', slot * 1024 + 512, 512),
                             in_=self.d_sin[:, c * 512:(c + 1) * 512]), 16)
            if not wkv_waited:
                T.wait_ge(s["sWKV"].h, 16)
                T.wait_ge(s["sCN"].h, s["sCN"].n)
                T.wait_ge(s["sCNG"].h, s["sCNG"].n)
                V.wait_ge(s["sCN"].h, s["sCN"].n)
                SC.wait_ge(s["sCN"].h, s["sCN"].n)
                wkv_waited = True
            T.wait_ge(s["sXT"].h, xt_n)
            for u in range(4):
                bank = u % 2
                lhs = [self.A('wkv', (u * 16 + f) * 128, 128) for f in range(16)]
                rhs = [xt_tile(f, slot) for f in range(16)]
                proj_accum(bank, lhs, rhs)
                kdst = self.A('kt', u * 4096 + c * 512, 512)
                self.rope_evac(kdst, self.bk[:, u:u + 1], bank, slot)
                pj_evac[bank] = (s["sKE"], s["sKE"].n)
            for half in range(2):
                wvsrc = self.d_wvt.rearrange("(f p) (u c) -> p f u c", p=128, c=128)[
                    :, half * 8:(half + 1) * 8]
                wvdst = self.A('wvh').rearrange("p (f u c) -> p f u c", f=8, u=4)
                G.wait_ge(s["sPJ"].h, self.wv_used)
                wv_n = s["sWV"].inc(G.dma_start(out=wvdst, in_=wvsrc), 16)
                T.wait_ge(s["sWV"].h, wv_n)
                wv_rr = self.A('wvh').rearrange("p (f u c) -> p f u c", f=8, u=4)
                for tt in range(4):
                    bank = 2 + tt
                    if half == 0 and pj_evac.get(bank) is not None:
                        sem, val = pj_evac[bank]
                        T.wait_ge(sem.h, val)
                    for f8 in range(8):
                        f = half * 8 + f8
                        last = (half == 1) and (f8 == 7)
                        ins = T.matmul(self.psum(bank), xt_tile(f, slot, tt * 128, 128),
                                       wv_rr[:, f8], start=(half == 0 and f8 == 0),
                                       stop=False)
                    if half == 1:
                        ins = T.matmul(self.psum(bank), self.C_('ones', 128, p0=0, p1=1),
                                       self.C_('bv', KVD, p0=0, p1=1),
                                       start=False, stop=True)
                    pj_n = s["sPJ"].inc(ins)
                self.wv_used = s["sPJ"].n
            for tt in range(4):
                bank = 2 + tt
                V.wait_ge(s["sPJ"].h, self.wv_used)
                ktile = c * 4 + tt
                vx_dst = self.A('vx').rearrange("p (a k e) -> p a k e", a=4, k=32)[:, :, ktile, 0:64]
                vy_dst = self.A('vy').rearrange("p (a k e) -> p a k e", a=4, k=32)[:, :, ktile, 64:128]
                ps_rr = self.psum(bank).rearrange("p (u e) -> p u e", u=4)
                V.tensor_copy(vx_dst, ps_rr[:, :, 0:64])
                ve_n = s["sVE"].inc(V.tensor_copy(vy_dst, ps_rr[:, :, 64:128]))
                pj_evac[bank] = (s["sVE"], ve_n)
            self.xt_free[slot] = s["sPJ"].n
            if c == 0:
                G.wait_ge(s["sCN"].h, s["sCN"].n)
                G.wait_ge(s["sCNG"].h, s["sCNG"].n)
                vx_rr = self.A('vx').rearrange("p (a k e) -> p a k e", a=4, k=32)
                vy_rr = self.A('vy').rearrange("p (a k e) -> p a k e", a=4, k=32)
                G.memset(vx_rr[:, :, :, 64:65], 1.0)
                G.memset(vy_rr[:, :, :, 32:33], 1.0)
                G.memset(vy_rr[:, :, :, 0:32], 0.0)
                s["sVE"].inc(G.memset(vy_rr[:, :, :, 33:64], 0.0))
            if c == 1:
                self.emit_q_stream(pj_evac, xt_tile)
                self.xt_free = [s["sPJ"].n, s["sPJ"].n]
                self.wv_used = s["sPJ"].n

    def emit_q_stream(self, pj_evac, xt_tile):
        nc, s = self.nc, self.s
        T, G = nc.tensor, nc.gpsimd
        G.wait_ge(s["sPJ"].h, s["sPJ"].n)
        for t in range(16):
            wslot = t % 2
            src = self.d_wqt.rearrange("(f p) (t2 c) -> p t2 f c", p=128, c=128)[:, t]
            dst = self.A('wqs', wslot * 2048, 2048).rearrange("p (f c) -> p f c", f=16)
            if t >= 2:
                G.wait_ge(s["sPJ"].h, self.wq_used[wslot])
            wq_n = s["sWQ"].inc(G.dma_start(out=dst, in_=src), 16)
            T.wait_ge(s["sWQ"].h, wq_n)
            for cc in range(2):
                bank = cc
                if pj_evac.get(bank) is not None:
                    sem, val = pj_evac[bank]
                    T.wait_ge(sem.h, val)
                for i in range(16):
                    ins = T.matmul(self.psum(bank),
                                   self.A('wqs', wslot * 2048 + i * 128, 128),
                                   xt_tile(i, cc), start=(i == 0), stop=(i == 15))
                s["sPJ"].inc(ins)
                qdst = self.A('qt', t * 1024 + cc * 512, 512)
                self.rope_evac(qdst, self.bq[:, t:t + 1], bank, cc)
                pj_evac[bank] = (s["sKE"], s["sKE"].n)
            self.wq_used[wslot] = s["sPJ"].n

    def _dump(self, pairs, waits):
        nc, s = self.nc, self.s
        SY = nc.sync
        for sem, val in waits:
            SY.wait_ge(sem.h, val)
        for dram, ap in pairs:
            s["sDBG"].inc(SY.dma_start(out=dram, in_=ap), 16)
        SY.wait_ge(s["sDBG"].h, s["sDBG"].n)

    def emit_dbg1(self):
        s = self.s
        self._dump([(self.d_dbg_kt, self.A('kt')), (self.d_dbg_vx, self.A('vx')),
                    (self.d_dbg_vy, self.A('vy')), (self.d_dbg_qt, self.A('qt'))],
                   [(s["sRC"], s["sRC"].n), (s["sVE"], s["sVE"].n)])

    def emit_dbg2(self):
        s = self.s
        self._dump([(self.d_dbg_yt, self.A('yt'))], [(s["aDONE"], 3)])

    def emit_dbg3(self):
        s = self.s
        self._dump([(self.d_dbg_z, self.Af('zacc')), (self.d_dbg_x1, self.A('x1tok')),
                    (self.d_dbg_x1t, self.A('x1T')),
                    (self.d_dbg_ln, self.Sf('ln', 0, 16))],
                   [(s["sLN"], s["sLN"].n), (s["sX1E"], s["sX1E"].n)])

    def emit_dbg4(self):
        s = self.s
        self._dump([(self.d_dbg_z, self.Af('zacc')),
                    (self.d_dbg_ln, self.Sf('tv', 0, 16)),
                    (self.d_dbg_hid, self.A('hidden'))],
                   [(s["sZ2"], s["sZ2"].n)])

    def emit_p2(self):
        nc, s = self.nc, self.s
        T, SC, V, SY = nc.tensor, nc.scalar, nc.vector, nc.sync
        rc_total, ve_total = s["sRC"].n, s["sVE"].n
        T.wait_ge(s["sRC"].h, rc_total)
        T.wait_ge(s["sVE"].h, ve_total)
        SC.wait_ge(s["sRC"].h, rc_total)
        V.wait_ge(s["sRC"].h, rc_total)
        CW = 1024
        self.gchunks = []
        for dram, nr, ncol in ((self.d_w1t, D, FH), (self.d_w2t, D, FH),
                               (self.d_w3t, FH, D)):
            for r0 in range(0, nr, 128):
                for c0 in range(0, ncol, CW):
                    self.gchunks.append((dram, r0, c0, min(CW, ncol - c0)))
        sqz0 = s["sQZ"].n
        SY.wait_ge(s["sVE"].h, ve_total)
        SY.wait_ge(s["sPJ"].h, s["sPJ"].n)
        for k, (dram, r0, c0, cw) in enumerate(self.gchunks):
            if k >= 2:
                SY.wait_ge(s["sQZ"].h, sqz0 + k - 1)
            s["sW12"].inc(SY.dma_start(out=self.Af('gma', (k % 2) * 1024, cw),
                                       in_=dram[r0:r0 + 128, c0:c0 + cw]), 16)
        snap0 = {k: sem.n for k, sem in self.s.items()}
        delta = None
        for name, eng in (('T', T), ('SC', SC), ('V', V)):
            with eng.register("j" + name) as reg:
                pid = nc.partition_id([eng.engine])
                eng.reg_mov(reg, pid)
                eng.reg_alu(reg, reg, 3, ALU.bitwise_and)
                for j in range(4):
                    for k, sem in self.s.items():
                        sem.n = snap0[k]
                    with eng.If_eq(reg, j):
                        self.emit_attention_body(j, name)
                    if delta is None:
                        delta = {k: sem.n - snap0[k] for k, sem in self.s.items()}
                eng.nop()
        for k, sem in self.s.items():
            sem.n = snap0[k] + delta[k]
        self.s["aDONE"].n = 3

    def emit_attention_body(self, j, active):
        nc, s = self.nc, self.s
        T0, SC0, V0 = nc.tensor, nc.scalar, nc.vector

        class _Null:
            def __getattr__(self, name):
                return lambda *a, **k: None

        NULL = _Null()
        T = T0 if active == 'T' else NULL
        SC = SC0 if active == 'SC' else NULL
        V = V0 if active == 'V' else NULL
        qk = {0: Sem(s["aQKx"].h), 1: Sem(s["aQKy"].h)}
        ex = {0: Sem(s["aEXx"].h), 1: Sem(s["aEXy"].h)}
        av = {0: Sem(s["aAVx"].h), 1: Sem(s["aAVy"].h)}
        mk, dn, bc, nm = (Sem(s["aMK"].h), Sem(s["aDN"].h),
                          Sem(s["aBC"].h), Sem(s["aNM"].h))
        perm = perm_blocks(j)
        bpos_blk = {blk: i for i, blk in enumerate(perm)}

        def bpos(kt_log):
            return bpos_blk[kt_log // 4] * 4 + kt_log % 4

        SCB = {0: 0, 1: 2}
        AVB = {0: 4, 1: 6}
        GSZ = 2
        neg12 = self.cf[:, 128:129]
        exp_at = {0: 0, 1: 0}
        nm_at = {0: [0, 0], 1: [0, 0]}
        slot_need = {0: [0, 0], 1: [0, 0]}
        tile_i = 0
        pending = None
        g_total = len(self.gchunks)
        units_total = 16 * 36
        g_done = [0]
        u_done = [0]
        sw12_0 = s["sW12"].n - 16 * g_total

        def gamma_abs(nu):
            u_done[0] += nu
            want = min(g_total, int(g_total * u_done[0] / units_total))
            while g_done[0] < want:
                k = g_done[0]
                dram, r0, c0, cw = self.gchunks[k]
                buf = self.Af('gma', (k % 2) * 1024, cw)
                SC.wait_ge(s["sW12"].h, sw12_0 + 16 * (k + 1))
                dummy = self.S_('sig', 0, cw)
                s["sQZ"].inc(SC.activation(dummy, buf, AF.Abs,
                                           accum_out=self.Sf('gbuf', k, 1)))
                g_done[0] += 1

        def epilogue_part2(p):
            t, qb, pav, dn_need = p
            for h in (0, 1):
                avb = AVB[h] + pav
                drow = 64 if h == 0 else 32
                yp0, yp1 = (0, 64) if h == 0 else (64, 128)
                denb = self.S_('kpre', 0, 512, p0=drow, p1=drow + 1).tensor if False else \
                    self.arn[drow:drow + 1, self.R['scr'][0] + 1024: self.R['scr'][0] + 1536]
                ones = self.C_('ones', 64, p0=drow, p1=drow + 1)
                T.wait_ge(dn.h, dn_need[h])
                if h == 0:
                    ins = T.matmul(self.psum(avb, p0=0, p1=64), ones, denb,
                                   start=True, stop=True)
                else:
                    ins = T.matmul(self.psum(avb, p0=64, p1=128), ones, denb,
                                   start=True, stop=True, tile_position=(32, 64))
                bc_n = bc.inc(ins)
                V.wait_ge(bc.h, bc_n)
                ysl = self.A('yt', t * 1024 + qb * 512, 512, p0=yp0, p1=yp1)
                ysc = self.S_('stt', 0, 512, p0=yp0, p1=yp1)
                nm.inc(V.tensor_tensor(ysl, ysc, self.psum(avb, p0=yp0, p1=yp1),
                                       op=ALU.mult))
                nm_at[h][pav] = nm.n

        for t in range(16):
            a = t // 4
            for qb in range(2):
                Lq = perm[qb]
                nkt = 4 * Lq + 4
                groups = [list(range(nkt))[i:i + GSZ] for i in range(0, nkt, GSZ)]
                p_av = tile_i % 2
                tile_i += 1
                first_av = {0: True, 1: True}
                av_bank_waited = False
                for gi, grp in enumerate(groups):
                    gp = gi % 2
                    for h in (0, 1):
                        T.wait_ge(ex[h].h, exp_at[h])
                    if not av_bank_waited:
                        for h in (0, 1):
                            T.wait_ge(nm.h, nm_at[h][p_av])
                        av_bank_waited = True
                    if gi == min(1, len(groups) - 1) and pending is not None:
                        epilogue_part2(pending)
                        pending = None
                    qk_ins = {}
                    for i, kt in enumerate(grp):
                        kb_ = bpos(kt)
                        for h in (0, 1):
                            p0, p1 = (0, 64) if h == 0 else (64, 128)
                            scb = SCB[h] + i
                            lhsT = self.A('kt', a * 4096 + kb_ * 128, 128, p0=p0, p1=p1)
                            rhs = self.A('qt', t * 1024 + qb * 512, 512, p0=p0, p1=p1)
                            qk_ins[h] = T.matmul(self.psum(scb), lhsT, rhs,
                                                 start=True, stop=True)
                    for h in (0, 1):
                        qk[h].inc(qk_ins[h])
                    for h in (0, 1):
                        SC.wait_ge(qk[h].h, qk[h].n)
                        if slot_need[h][gp]:
                            SC.wait_ge(av[h].h, slot_need[h][gp])
                        ebase = h * 2048 + gp * 1024
                        eout = self.A('expx', ebase, 1024)
                        ex_n = ex[h].inc(SC.activation(
                            eout, self.psum(SCB[h], n_f32=1024),
                            AF.Exp, bias=neg12, scale=0.125))
                        exp_at[h] = ex_n
                        mk_n0 = mk.n
                        for i, kt in enumerate(grp):
                            r = kt - 4 * Lq
                            if r >= 0:
                                V.wait_ge(ex[h].h, ex_n)
                                mrow = self.C_('masks', 512, off=r * 512)
                                esl = self.A('expx', ebase + i * 512, 512)
                                mk.inc(V.tensor_tensor(esl, esl, mrow, op=ALU.mult))
                        T.wait_ge(ex[h].h, ex_n)
                        if mk.n > mk_n0:
                            T.wait_ge(mk.h, mk.n)
                        for i, kt in enumerate(grp):
                            kb_ = bpos(kt)
                            if h == 0:
                                lhsT = self.A('vx', a * 2112 + kb_ * 66, 65)
                                avout = self.psum(AVB[h] + p_av, p0=0, p1=65)
                            else:
                                lhsT = self.A('vy', a * 4096 + kb_ * 128, 128)
                                avout = self.psum(AVB[h] + p_av)
                            rhs = self.A('expx', ebase + i * 512, 512)
                            ins = T.matmul(avout, lhsT, rhs,
                                           start=first_av[h],
                                           stop=(gi == len(groups) - 1 and i == len(grp) - 1))
                            first_av[h] = False
                        av_n = av[h].inc(ins)
                        slot_need[h][gp] = av_n
                gamma_abs(nkt)
                dn_need = {}
                V.wait_ge(bc.h, bc.n)
                for h in (0, 1):
                    avb = AVB[h] + p_av
                    drow = 64 if h == 0 else 32
                    yp0, yp1 = (0, 64) if h == 0 else (64, 128)
                    denf = self.Sf('kpre', 0, 512, p0=drow, p1=drow + 1)
                    dsc = self.Sf('m12', 0, 512, p0=drow, p1=drow + 1)
                    denb = self.arn[drow:drow + 1,
                                    self.R['scr'][0] + 1024: self.R['scr'][0] + 1536]
                    ysc = self.S_('stt', 0, 512, p0=yp0, p1=yp1)
                    V.wait_ge(av[h].h, av[h].n)
                    V.tensor_copy(ysc, self.psum(avb, p0=yp0, p1=yp1))
                    V.tensor_copy(denf, self.psum(avb, p0=drow, p1=drow + 1))
                    V.reciprocal(denf, denf)
                    dn_need[h] = dn.inc(V.tensor_copy(denb, denf))
                pending = (t, qb, p_av, dn_need)
        gamma_abs(units_total)
        epilogue_part2(pending)
        if active == 'T':
            T.wait_ge(nm.h, nm.n)
            T.matmul(self.psum(0, n_f32=1, p0=0, p1=1), self.C_('ones', 1, p0=0, p1=1),
                     self.C_('ones', 1, p0=0, p1=1), start=True, stop=True,
                     skip_group_check=True).then_inc(s["aDONE"].h, 1)
        elif active == 'SC':
            SC.activation(self.S_('stt', 512, 1, p0=0, p1=1),
                          self.S_('stt', 512, 1, p0=0, p1=1), AF.Identity,
                          bias=0.0, scale=1.0).then_inc(s["aDONE"].h, 1)
        else:
            V.tensor_copy(self.S_('stt', 514, 1, p0=0, p1=1),
                          self.S_('stt', 512, 1, p0=0, p1=1)).then_inc(s["aDONE"].h, 1)

    def emit_p3(self):
        nc, s = self.nc, self.s
        T, SC, V, SY, G = nc.tensor, nc.scalar, nc.vector, nc.sync, nc.gpsimd
        SY.wait_ge(s["aDONE"].h, 3)
        zdst = self.Af('zacc').rearrange("p (ss c) -> p ss c", ss=8)
        zsrc = self.d_xo.rearrange("(ss p) c -> p ss c", p=128)
        zi_n = s["sZI"].inc(SY.dma_start(out=zdst, in_=zsrc), 16)

        G.wait_ge(s["aDONE"].h, 3)
        op_evac = {}
        for oc in range(4):
            wsrc = self.d_wot.rearrange("(f p) c -> p f c", p=128)[:, :, oc * 512:(oc + 1) * 512]
            wdst = self.A('woT').rearrange("p (f c) -> p f c", f=16)
            if oc > 0:
                G.wait_ge(s["sOP"].h, s["sOP"].n)
            wo_n = s["sWO"].inc(G.dma_start(out=wdst, in_=wsrc), 16)
            T.wait_ge(s["sWO"].h, wo_n)
            if oc == 0:
                T.wait_ge(s["aDONE"].h, 3)
            for sidx in range(8):
                bank = sidx % 2
                if op_evac.get(bank) is not None:
                    T.wait_ge(s["sZA"].h, op_evac[bank])
                for tt in range(16):
                    T.matmul(self.psum(bank), self.A('yt', tt * 1024 + sidx * 128, 128),
                             self.A('woT', tt * 512, 512), start=(tt == 0), stop=False)
                ins = T.matmul(self.psum(bank), self.C_('ones', 128, p0=0, p1=1),
                               self.C_('bo', 512, off=oc * 512, p0=0, p1=1),
                               start=False, stop=True)
                op_n = s["sOP"].inc(ins)
                V.wait_ge(s["sOP"].h, op_n)
                V.wait_ge(s["sZI"].h, zi_n)
                zsl = self.Af('zacc', sidx * 2048 + oc * 512, 512)
                za_n = s["sZA"].inc(V.tensor_tensor(zsl, zsl, self.psum(bank), op=ALU.add))
                op_evac[bank] = za_n

        V.wait_ge(s["sZA"].h, s["sZA"].n)
        x1e = {}
        for sidx in range(8):
            self.emit_ln(self.Af('zacc', sidx * 2048, 2048),
                         self.A('x1tok', sidx * 2048, 2048), s["sLN"])
            ln_n = s["sLN"].n
            T.wait_ge(s["sLN"].h, ln_n)
            for half in range(2):
                bank = 2 + half
                if x1e.get(bank) is not None:
                    T.wait_ge(s["sX1E"].h, x1e[bank])
                for f8 in range(8):
                    f = half * 8 + f8
                    ins = T.transpose(self.psum_bf(bank, 128, off_bf=f8 * 128),
                                      self.A('x1tok', sidx * 2048 + f * 128, 128),
                                      self.C_('ident', 128))
                op_n = s["sOP"].inc(ins)
                V.wait_ge(s["sOP"].h, op_n)
                dst = self.A('x1T').rearrange("p (f c) -> p f c", f=16)[
                    :, half * 8:(half + 1) * 8, sidx * 128:(sidx + 1) * 128]
                src = self.psum_bf(bank, 1024).rearrange("p (f c) -> p f c", f=8)
                x1e[bank] = s["sX1E"].inc(V.tensor_copy(dst, src))

    def emit_ln(self, src_f32, out_ap, sem):
        nc = self.nc
        SC, V = nc.scalar, nc.vector
        n_free = src_f32.shape[-1]
        sum_ = self.Sf('ln', 0, 1)
        nmean = self.Sf('ln', 1, 1)
        var = self.Sf('ln', 2, 1)
        m2 = self.Sf('ln', 3, 1)
        rstd = self.Sf('ln', 4, 1)
        sq = self.Sf('ln', 5, 1)
        dummy = self.Af('lnd', 0, n_free)
        V.tensor_reduce(sum_, src_f32, axis=AX.X, op=ALU.add)
        V.scalar_tensor_tensor(dummy, src_f32, 1.0, src_f32, op0=ALU.mult,
                               op1=ALU.mult, accum_out=sq)
        n1 = sem.inc(V.tensor_scalar(nmean, sum_, -1.0 / n_free, None, op0=ALU.mult))
        V.wait_ge(sem.h, n1)
        n2 = sem.inc(V.tensor_scalar(m2, nmean, nmean, None, op0=ALU.mult))
        V.wait_ge(sem.h, n2)
        n2b = sem.inc(V.tensor_scalar(var, sq, 1.0 / n_free, m2,
                                      op0=ALU.mult, op1=ALU.subtract))
        V.wait_ge(sem.h, n2b)
        vn = sem.inc(V.tensor_scalar(var, var, EPS_LN, None, op0=ALU.add))
        SC.wait_ge(sem.h, vn)
        sn = sem.inc(SC.activation(rstd, var, AF.Sqrt))
        V.wait_ge(sem.h, sn)
        n3 = sem.inc(V.reciprocal(rstd, rstd))
        V.wait_ge(sem.h, n3)
        sem.inc(V.tensor_scalar(out_ap, src_f32, nmean, rstd, op0=ALU.add, op1=ALU.mult))

    def emit_p4(self):
        nc, s = self.nc, self.s
        T, SC, V, SY = nc.tensor, nc.scalar, nc.vector, nc.sync
        ln_total = s["sLN"].n
        SY.wait_ge(s["sLN"].h, ln_total)
        SC.wait_ge(s["sLN"].h, ln_total)
        SC.wait_ge(s["sX1E"].h, s["sX1E"].n)

        CW_G = 1024
        gs = []
        pos = 0
        for nr, ncol in ((D, FH), (D, FH), (FH, D)):
            cnt = (nr // 128) * ((ncol + CW_G - 1) // CW_G)
            gs.append(list(range(pos, pos + cnt)))
            pos += cnt
        V.wait_ge(s["sQZ"].h, s["sQZ"].n)
        qz_n = None
        for mi in range(3):
            qz_n = s["sQZ"].inc(V.tensor_reduce(
                self.Sf('ln', 5 + mi, 1),
                self.Sf('gbuf', gs[mi][0], len(gs[mi])), axis=AX.X, op=ALU.add))
        V.wait_ge(s["sQZ"].h, qz_n)
        T.wait_ge(s["sQZ"].h, qz_n)
        fp_n = s["sFP"].inc(T.matmul(self.psum(0, n_f32=3, p0=0, p1=1), self.cf[:, 0:1],
                                     self.Sf('ln', 5, 3), start=True, stop=True))
        V.wait_ge(s["sFP"].h, fp_n)
        NW = float(D) * float(FH)
        trow = self.Sf('tv', 6, 3, p0=0, p1=1)
        qz_n = s["sQZ"].inc(V.tensor_scalar(trow, self.psum(0, n_f32=3, p0=0, p1=1),
                                            0.5 / NW, 0.5 * EPS_Q, op0=ALU.mult, op1=ALU.add))
        T.wait_ge(s["sQZ"].h, qz_n)
        fp_n = s["sFP"].inc(T.matmul(self.psum(1, n_f32=3), self.cf[0:1, 0:128],
                                     trow, start=True, stop=True))
        V.wait_ge(s["sFP"].h, fp_n)
        tvec = self.Sf('tv', 0, 3)
        ntvec = self.Sf('tv', 3, 3)
        qz_n = s["sQZ"].inc(V.tensor_copy(tvec, self.psum(1, n_f32=3)))
        V.wait_ge(s["sQZ"].h, qz_n)
        qz_n = s["sQZ"].inc(V.tensor_scalar(ntvec, tvec, -1.0, None, op0=ALU.mult))
        tv_ready = qz_n

        def quantize(dst_bf, src_f32, mi):
            ncols = src_f32.shape[-1]
            last = None
            for c0 in range(0, ncols, 512):
                cw = min(512, ncols - c0)
                Bp = self.psum(6, n_f32=cw)
                V.tensor_scalar(Bp, src_f32[:, c0:c0 + cw],
                                self.Sf('tv', 3 + mi, 1), None, op0=ALU.is_lt)
                last = V.scalar_tensor_tensor(
                    dst_bf[:, c0:c0 + cw], src_f32[:, c0:c0 + cw],
                    self.Sf('tv', mi, 1), Bp, op0=ALU.is_gt, op1=ALU.subtract)
            return s["sQZ"].inc(last)

        V.wait_ge(s["sQZ"].h, tv_ready)
        zi2 = None
        for sidx in range(8):
            zi2 = s["sZ2"].inc(V.tensor_copy(self.Af('zacc', sidx * 2048, 2048),
                                             self.A('x1tok', sidx * 2048, 2048)))
        group_sizes = [8, 8, 8, 8, 8, 4]
        gstart = 0
        w12_dma_gate = 0
        hp_evac = {}
        fp_evac = {}
        wq_mm_done = [0, 0]
        hid_used = 0
        self.w3_qz_at = [0, 0]
        self.w3_mm_at = [0, 0]
        self.sig_used = [0, 0]

        def w12q_slot(h):
            if h % 2 == 0:
                return (self.A('w12q', 0, 2048), self.A('w12q', 2048, 2048))
            return (self.S_('w12qB', 0, 2048), self.S_('w12qB', 2048, 2048))

        for g, gsz in enumerate(group_sizes):
            for hs in range(gsz):
                h = gstart + hs
                if w12_dma_gate:
                    SY.wait_ge(s["sQZ"].h, w12_dma_gate)
                n1 = s["sW12"].inc(SY.dma_start(
                    out=self.Af('w12f', 0, 2048).rearrange("p (f c) -> p f c", f=16),
                    in_=self.d_w1t.rearrange("(f p) m -> p f m", p=128)[:, :, h * 128:(h + 1) * 128]), 16)
                n2 = s["sW12"].inc(SY.dma_start(
                    out=self.Af('w12f', 2048, 2048).rearrange("p (f c) -> p f c", f=16),
                    in_=self.d_w2t.rearrange("(f p) m -> p f m", p=128)[:, :, h * 128:(h + 1) * 128]), 16)
                V.wait_ge(s["sW12"].h, n2)
                V.wait_ge(s["sQZ"].h, tv_ready)
                if wq_mm_done[h % 2]:
                    V.wait_ge(s["sHP"].h, wq_mm_done[h % 2])
                if self.w3_mm_at[h % 2]:
                    V.wait_ge(s["sFP"].h, self.w3_mm_at[h % 2])
                dstA, dstB = w12q_slot(h)
                qz_w1 = quantize(dstA, self.Af('w12f', 0, 2048), 0)
                qz_after = quantize(dstB, self.Af('w12f', 2048, 2048), 1)
                w12_dma_gate = qz_after
                hp_done = {}
                for mi in range(2):
                    src = dstA if mi == 0 else dstB
                    T.wait_ge(s["sQZ"].h, qz_w1 if mi == 0 else qz_after)
                    for cc in range(2):
                        bank = mi * 2 + cc
                        if hp_evac.get(bank) is not None:
                            sem, val = hp_evac[bank]
                            T.wait_ge(sem.h, val)
                        for i in range(16):
                            ins = T.matmul(self.psum(bank),
                                           src[:, i * 128:(i + 1) * 128],
                                           self.A('x1T', i * 1024 + cc * 512, 512),
                                           start=(i == 0), stop=(i == 15))
                        hp_done[bank] = s["sHP"].inc(ins)
                wq_mm_done[h % 2] = s["sHP"].n
                for cc in range(2):
                    b1, b2 = cc, 2 + cc
                    sig = self.Sf('sig', 0, 512) if cc == 0 else self.Sf('m12', 0, 512)
                    SC.wait_ge(s["sHP"].h, hp_done[b1])
                    SC.wait_ge(s["sHD"].h, self.sig_used[cc])
                    if hid_used:
                        SC.wait_ge(s["sFP"].h, hid_used)
                    sg_n = s["sSG"].inc(SC.activation(sig, self.psum(b1), AF.Sigmoid))
                    V.wait_ge(s["sSG"].h, sg_n)
                    V.wait_ge(s["sHP"].h, hp_done[b2])
                    hd_n = s["sHD"].inc(V.tensor_tensor(
                        self.A('hidden', hs * 1024 + cc * 512, 512),
                        sig, self.psum(b2), op=ALU.mult))
                    self.sig_used[cc] = hd_n
                    hp_evac[b1] = (s["sSG"], sg_n)
                    hp_evac[b2] = (s["sHD"], hd_n)
            hd_total = s["sHD"].n
            T.wait_ge(s["sHD"].h, hd_total)
            for oc in range(4):
                w3f = self.Af('w12f', 0, gsz * 512)
                w3q = self.A('w12q', 0, gsz * 512)
                SY.wait_ge(s["sZ2"].h, zi2)
                SY.wait_ge(s["sQZ"].h, s["sQZ"].n)
                src = self.d_w3t[gstart * 128:(gstart + gsz) * 128,
                                 oc * 512:(oc + 1) * 512]
                w3_n = s["sW3"].inc(SY.dma_start(
                    out=w3f.rearrange("p (hs c) -> p hs c", hs=gsz),
                    in_=src.rearrange("(hs p) c -> p hs c", p=128)), 16)
                V.wait_ge(s["sW3"].h, w3_n)
                if self.w3_mm_at[0]:
                    V.wait_ge(s["sFP"].h, self.w3_mm_at[0])
                V.wait_ge(s["sHP"].h, wq_mm_done[0])
                q3_total = quantize(w3q, w3f, 2)
                T.wait_ge(s["sQZ"].h, q3_total)
                for sidx in range(8):
                    bank = (4, 5, 7)[sidx % 3]
                    if fp_evac.get(bank) is not None:
                        T.wait_ge(s["sZ2"].h, fp_evac[bank])
                    for hs in range(gsz):
                        ins = T.matmul(self.psum(bank),
                                       self.A('hidden', hs * 1024 + sidx * 128, 128),
                                       w3q[:, hs * 512:(hs + 1) * 512],
                                       start=(hs == 0), stop=(hs == gsz - 1))
                    fp_n = s["sFP"].inc(ins)
                    V.wait_ge(s["sFP"].h, fp_n)
                    zsl = self.Af('zacc', sidx * 2048 + oc * 512, 512)
                    z2_n = s["sZ2"].inc(V.tensor_tensor(
                        zsl, zsl, self.psum(bank), op=ALU.add))
                    fp_evac[bank] = z2_n
                self.w3_mm_at[0] = s["sFP"].n
            hid_used = s["sFP"].n
            w12_dma_gate = s["sQZ"].n
            gstart += gsz

    def emit_p5(self):
        nc, s = self.nc, self.s
        SY, V = nc.sync, nc.vector
        z2_total = s["sZ2"].n
        V.wait_ge(s["sZ2"].h, z2_total)
        out_used = [0, 0]
        for sidx in range(8):
            slot = sidx % 2
            obuf = self.Af('outs', 0, 2048) if slot == 0 else self.Af('w1r', 2048, 2048)
            if out_used[slot]:
                V.wait_ge(s["sOUT"].h, out_used[slot])
            self.emit_ln(self.Af('zacc', sidx * 2048, 2048), obuf, s["sL2"])
            l2_n = s["sL2"].n
            SY.wait_ge(s["sL2"].h, l2_n)
            ins = SY.dma_start(out=self.d_out[sidx * 128:(sidx + 1) * 128, :], in_=obuf)
            out_used[slot] = s["sOUT"].inc(ins, 16)
        SY.wait_ge(s["sOUT"].h, s["sOUT"].n)


_BUILD_CACHE = {}


def _get_program(dbg=0):
    if dbg not in _BUILD_CACHE:
        kb = KB(dbg)
        _BUILD_CACHE[dbg] = (kb.build(), kb)
    return _BUILD_CACHE[dbg]


def make_consts():
    ident = np.eye(128, dtype=ml_dtypes.bfloat16)
    P = np.zeros((128, 128), dtype=ml_dtypes.bfloat16)
    for base in (0, 64):
        for dd in range(32):
            P[base + dd + 32, base + dd] = 1
            P[base + dd, base + dd + 32] = 1
    masks = np.zeros((128, 4 * 512), dtype=ml_dtypes.bfloat16)
    p = np.arange(128)[:, None]
    q = np.arange(512)[None, :]
    for r in range(4):
        masks[:, r * 512:(r + 1) * 512] = (q >= 128 * r + p).astype(ml_dtypes.bfloat16)
    ones = np.ones((128, 128), dtype=ml_dtypes.bfloat16)
    cbf = np.ascontiguousarray(np.concatenate([ident, P, masks, ones], axis=1).astype(np.float32))
    cf32 = np.zeros((128, 132), dtype=np.float32)
    cf32[:, 0:128] = 1.0
    cf32[:, 128] = -12.0
    return cbf, cf32


def make_core_inputs(inputs, cid):
    x = np.asarray(inputs["L0_attn_x"])
    batch, j = cid // 4, cid % 4
    perm = perm_blocks(j)
    xb = x[batch]
    x_perm = np.concatenate([xb[b * BLK:(b + 1) * BLK] for b in perm], axis=0)
    xt = np.ascontiguousarray(x_perm.T.astype(np.float32))
    xo = np.ascontiguousarray(x_perm[:NTOK].astype(np.float32))
    jj = np.arange(32, dtype=np.float64)
    theta = 10000.0 ** (-2.0 * jj / 64.0)
    tok = np.concatenate([np.arange(b * BLK, (b + 1) * BLK) for b in perm])
    ang = (tok[None, :] + 1.0) * theta[np.arange(128) % 32][:, None]
    cos_t = np.cos(ang).astype(np.float32)
    sgn = np.where((np.arange(128) % 64) < 32, -1.0, 1.0)[:, None]
    sin_t = (np.sin(ang) * sgn).astype(np.float32)
    return dict(xt=xt, xo=xo, cos_t=np.ascontiguousarray(cos_t),
                sin_t=np.ascontiguousarray(sin_t))


def make_shared_inputs(inputs):
    qp = qrowperm()
    gi = lambda k: np.asarray(inputs[k])
    d = dict(
        wqt=np.ascontiguousarray(gi("L0_attn_Wq")[qp].T.astype(np.float32)),
        wkt=np.ascontiguousarray(gi("L0_attn_Wk").T.astype(np.float32)),
        wvt=np.ascontiguousarray(gi("L0_attn_Wv").T.astype(np.float32)),
        wot=np.ascontiguousarray(gi("L0_attn_Wo")[:, qp].T.astype(np.float32)),
        w1t=np.ascontiguousarray(gi("L0_ffn_w1").T.astype(np.float32)),
        w2t=np.ascontiguousarray(gi("L0_ffn_w2").T.astype(np.float32)),
        w3t=np.ascontiguousarray(gi("L0_ffn_w3").T.astype(np.float32)),
        bq_t=np.ascontiguousarray(gi("L0_attn_bq")[qp].reshape(16, 128).T.astype(np.float32)),
        bk_t=np.ascontiguousarray(gi("L0_attn_bk").reshape(4, 128).T.astype(np.float32)),
        bo_r=np.ascontiguousarray(gi("L0_attn_bo").reshape(1, D).astype(np.float32)),
        bv_r=np.ascontiguousarray(gi("L0_attn_bv").reshape(1, KVD).astype(np.float32)),
    )
    cbf, cf32 = make_consts()
    d["consts_bf"] = cbf
    d["cf32"] = cf32
    return d


def kernel(**inputs):
    nc, kb = _get_program(0)
    shared = make_shared_inputs(inputs)
    in_maps = []
    for cid in range(8):
        m = dict(shared)
        m.update(make_core_inputs(inputs, cid))
        in_maps.append(m)
    res = run_bass_kernel_spmd(nc, in_maps, core_ids=list(range(8)))
    out = np.zeros((B, T, D), dtype=np.float32)
    for cid in range(8):
        batch, j = cid // 4, cid % 4
        o = res.results[cid]["out"]
        out[batch, j * BLK:(j + 1) * BLK] = o[:BLK]
        out[batch, (7 - j) * BLK:(8 - j) * BLK] = o[BLK:]
    return out
